# version 17
# baseline (speedup 1.0000x reference)
from contextlib import ExitStack

import numpy as np

import concourse.bass as bass
import concourse.mybir as mybir
from concourse.bass_utils import run_bass_kernel_spmd

F32 = mybir.dt.float32
BF16 = mybir.dt.bfloat16
I32 = mybir.dt.int32
AF = mybir.ActivationFunctionType
ALU = mybir.AluOpType
AX = mybir.AxisListType

NCORES = 8
T = 2048
P = 128
NT = T // P
NB = 4
D = 1024
DC = D // P
HD = 64
INNER = 4096
NOG = 8
NJC = INNER // P
LN_EPS = 1e-5
EPS = 1e-6

ENGS = ("sync", "gpsimd", "vector", "scalar", "tensor")


def _finish(nc, block, ops, barrier, cleanup=None):
    if cleanup is not None:
        cleanup()
    @block.sync
    def _(e):
        for f in ops["sync"]:
            f(e)

    @block.gpsimd
    def _(e):
        for f in ops["gpsimd"]:
            f(e)

    @block.vector
    def _(e):
        for f in ops["vector"]:
            f(e)

    @block.scalar
    def _(e):
        for f in ops["scalar"]:
            f(e)

    @block.tensor
    def _(e):
        for f in ops["tensor"]:
            f(e)
    return nc


class Ctr:
    def __init__(self, nc, stack, name):
        self.h = stack.enter_context(nc.semaphore(name))
        self.n = 0

    def bump(self, k=1):
        self.n += k
        return self.n


class DSem:

    def __init__(self, nc, stack, name, k=1):
        self.sems = [stack.enter_context(nc.semaphore(f"{name}{i}")) for i in range(k)]
        self.n = [0] * k
        self.k = k

    def issue(self, em, eng, fn, slot=0):
        s = slot % self.k
        em(eng, lambda e, f=fn, h=self.sems[s]: f(e).then_inc(h, 16))
        self.n[s] += 16
        return (s, self.n[s])

    def wait(self, em, eng, token):
        if token is None:
            return
        s, v = token
        if v > 0:
            em(eng, lambda e, h=self.sems[s], vv=v: e.wait_ge(h, vv))

    def wait_all(self, em, eng):
        for s in range(self.k):
            if self.n[s] > 0:
                em(eng, lambda e, h=self.sems[s], vv=self.n[s]: e.wait_ge(h, vv))


def build(debug=False, stop_after=None):
    nc = bass.Bass("TRN2", target_bir_lowering=False, debug=False,
                   detect_race_conditions=False)

    dt_x = nc.dram_tensor("x", [T, D], F32, kind="ExternalInput").ap()
    dt_qkvwT = nc.dram_tensor("qkvwT", [D, 3 * D], F32, kind="ExternalInput").ap()
    dt_qkvb = nc.dram_tensor("qkv_b", [1, 3 * D], F32, kind="ExternalInput").ap()
    dt_qkbc = nc.dram_tensor("qkb_cols", [P, 24], F32, kind="ExternalInput").ap()
    dt_projwT = nc.dram_tensor("projwT", [D, D], F32, kind="ExternalInput").ap()
    dt_projb = nc.dram_tensor("proj_b", [1, D], F32, kind="ExternalInput").ap()
    dt_pmT = nc.dram_tensor("pmT", [HD, HD], F32, kind="ExternalInput").ap()
    dt_wgm = nc.dram_tensor("wg_mine", [P, INNER], F32, kind="ExternalInput").ap()
    dt_wum = nc.dram_tensor("wu_mine", [P, INNER], F32, kind="ExternalInput").ap()
    dt_wdm = nc.dram_tensor("wd_mine", [4 * P, D], F32, kind="ExternalInput").ap()
    dt_out = nc.dram_tensor("out", [T, D], F32, kind="ExternalOutput").ap()

    ik = "ExternalOutput" if debug else "Internal"
    d_xn = nc.dram_tensor("xn_d", [T, D], BF16, kind=ik).ap()
    d_xq = nc.dram_tensor("xq_d", [T, D], BF16, kind=ik).ap()
    d_qp = nc.dram_tensor("qp_d", [T, D], BF16, kind=ik).ap()
    d_a = nc.dram_tensor("a_d", [T, D], BF16, kind=ik).ap()
    d_h = nc.dram_tensor("h_d", [T, INNER], F32, kind="Internal").ap()
    d_hq = nc.dram_tensor("hq_d", [T, INNER], BF16, kind=ik).ap()
    d_wqg = nc.dram_tensor("wqg_full", [D, INNER], BF16, kind="Internal",
                           addr_space="Shared").ap()
    d_wqu = nc.dram_tensor("wqu_d", [D, INNER], BF16, kind="Internal",
                           addr_space="Shared").ap()
    d_wqd = nc.dram_tensor("wqd_d", [INNER, D], BF16, kind="Internal",
                           addr_space="Shared").ap()
    d_wqg_m = nc.dram_tensor("wqg_m", [P, INNER], BF16, kind="Internal").ap()
    d_wqu_m = nc.dram_tensor("wqu_m", [P, INNER], BF16, kind="Internal").ap()
    d_wqd_m = nc.dram_tensor("wqd_m", [4 * P, D], BF16, kind="Internal").ap()
    d_wqgdump = (nc.dram_tensor("wqg_d", [D, INNER], BF16, kind=ik).ap()
                 if debug else None)
    CCW = 1035
    d_ccin = nc.dram_tensor("cc_in", [P, CCW], F32, kind="Internal").ap()
    d_ccout = nc.dram_tensor("cc_out", [P, CCW], F32, kind="Internal").ap()
    d_ccdump = nc.dram_tensor("cc_dump", [P, CCW], F32, kind=ik).ap()

    ops = {e: [] for e in ENGS}

    def em(eng, fn):
        ops[eng].append(fn)

    def emi(eng, fn, ctr, k=1):
        em(eng, lambda e, f=fn, h=ctr.h, kk=k: f(e).then_inc(h, kk))
        return ctr.bump(k)

    def vsp():
        em("vector", lambda e: e.tensor_copy(spacer_t[:, 0:1], spacer_t[:, 1:2]))

    def wait(eng, ctr, val):
        if val > 0:
            em(eng, lambda e, h=ctr.h, v=val: e.wait_ge(h, v))

    with ExitStack() as stk:
        bar = Ctr(nc, stk, "bar")
        cv = Ctr(nc, stk, "cv")
        ca = Ctr(nc, stk, "ca")
        ct = Ctr(nc, stk, "ct")
        cg = Ctr(nc, stk, "cg")
        ccs = Ctr(nc, stk, "ccs")
        dss = []

        def mkds(name, k=1, eng="sync"):
            d = DSem(nc, stk, name, k)
            dss.append((d, eng))
            return d

        ds_p0 = mkds("dp0")
        ds_x = mkds("dx", 2, "sync")
        ds_tr = mkds("dtr", 2)
        ds_wt = mkds("dwt", 2)
        ds_fm1 = mkds("dfm1")
        ds_st = mkds("dst", 2)
        ds_h2 = mkds("dh2", 2)
        ds_fm2 = mkds("dfm2")
        ds_wd = mkds("dwd")
        ds_b2 = mkds("db2", 2)
        ds_x2 = mkds("dx2", 2)
        ds_qkvw = mkds("dqkvw", 1, "gpsimd")
        ds_xn = mkds("dxn", 2, "sync")
        ds_xq = mkds("dxq", 2, "sync")
        ds_qp = mkds("dqp", 2, "sync")
        ds_cc = mkds("dcc", 1, "gpsimd")
        ds_wq = mkds("dwq", 2, "gpsimd")
        ds_h = mkds("dh", 2, "sync")
        ds_hq = mkds("dhq", 2, "sync")
        ds_a = mkds("da", 2, "sync")
        ds_pj = mkds("dpj", 1, "gpsimd")
        ds_out = mkds("dout", 2, "sync")
        _sem_ids = [c.h.num for c in (bar, cv, ca, ct, cg, ccs)]
        for d, _ in dss:
            _sem_ids += [s.num for s in d.sems]
        _sem_range = range(min(_sem_ids), max(_sem_ids) + 1)
        em("gpsimd", lambda e, r=_sem_range: e.sem_clear(r))
        for _e in ENGS:
            if _e != "gpsimd":
                for _ in range(1024):
                    em(_e, lambda en: en.nop())
            em(_e, lambda en, h=bar.h: en.sem_inc(h, 1))
        bar.n += len(ENGS)
        _tgt0 = bar.n
        for _e in ENGS:
            wait(_e, bar, _tgt0)

        def barrier():
            for d, deng in dss:
                d.wait_all(em, deng)
            for e in ENGS:
                em(e, lambda en, h=bar.h: en.sem_inc(h, 1))
            bar.n += len(ENGS)
            tgt = bar.n
            for e in ENGS:
                wait(e, bar, tgt)

        def _cleanup():
            barrier()
            em("gpsimd", lambda e, r=_sem_range: e.dma_reset(r))
            em("gpsimd", lambda e, r=_sem_range: e.sem_clear(r))

        consts = stk.enter_context(nc.sbuf_tensor("consts", [P, 8], F32))
        CEPSLN = 1
        ones1 = stk.enter_context(nc.sbuf_tensor("ones1", [1, P], F32))
        ones128 = stk.enter_context(nc.sbuf_tensor("ones128", [P, 1], F32))
        hb = stk.enter_context(nc.sbuf_tensor("hb", [P, 2], BF16))
        bdpm = stk.enter_context(nc.sbuf_tensor("bdpm", [P, P], BF16))
        qkbs = stk.enter_context(nc.sbuf_tensor("qkbs", [P, 16], F32))
        vbias = stk.enter_context(nc.sbuf_tensor("vbias", [P, D], F32))
        pbias = stk.enter_context(nc.sbuf_tensor("pbias", [P, D], F32))
        rs_a = stk.enter_context(nc.sbuf_tensor("rs_a", [P, NT], F32))
        hmax = stk.enter_context(nc.sbuf_tensor("hmax", [P, NT], F32))
        sh_all = stk.enter_context(nc.sbuf_tensor("sh_all", [P, NT], F32))
        cgate = stk.enter_context(nc.sbuf_tensor("cgate", [P, NT], F32))
        cup = stk.enter_context(nc.sbuf_tensor("cup", [P, NT], F32))
        cdown = stk.enter_context(nc.sbuf_tensor("cdown", [P, NT], F32))
        scr16 = stk.enter_context(nc.sbuf_tensor("scr16", [P, NT], F32))
        ws_bc = stk.enter_context(nc.sbuf_tensor("ws_bc", [P, 3], F32))
        rw_bc = stk.enter_context(nc.sbuf_tensor("rw_bc", [P, 3], F32))
        ws11 = stk.enter_context(nc.sbuf_tensor("ws11", [1, 4], F32))
        kvr = stk.enter_context(nc.sbuf_tensor("kvr", [P, 8 * 130], BF16))
        spacer_t = stk.enter_context(nc.sbuf_tensor("spacer_t", [P, 2], F32))
        cc_sb = stk.enter_context(nc.sbuf_tensor("cc_sb", [P, CCW], F32))

        block = stk.enter_context(nc.Block())

        g_ccback = 0

        with ExitStack() as p0s:
            en = p0s.enter_context
            qb_row = en(nc.sbuf_tensor("qb_row", [1, 3 * D], F32))
            pb_row = en(nc.sbuf_tensor("pb_row", [1, D], F32))
            qkbc = en(nc.sbuf_tensor("qkbc", [P, 24], F32))
            pm2 = en(nc.sbuf_tensor("pm2", [P, HD], F32))
            pm2b = en(nc.sbuf_tensor("pm2b", [P, HD], BF16))
            ps_b1 = en(nc.psum_tensor("ps_b1", [P, 512], F32))
            ps_b2 = en(nc.psum_tensor("ps_b2", [P, 512], F32))

            ds_p0.issue(em, "sync", lambda e: e.dma_start(qb_row[:, :], dt_qkvb[:, :]))
            ds_p0.issue(em, "sync", lambda e: e.dma_start(pb_row[:, :], dt_projb[:, :]))
            ds_p0.issue(em, "sync", lambda e: e.dma_start(qkbc[:, :], dt_qkbc[:, :]))
            ds_p0.issue(em, "sync", lambda e: e.dma_start(pm2[0:64, :], dt_pmT[:, :]))
            ds_p0.issue(em, "sync", lambda e: e.dma_start(pm2[64:128, :], dt_pmT[:, :]))

            em("vector", lambda e: e.memset(spacer_t[:, :], 0.0))
            em("vector", lambda e: e.memset(consts[:, :], 0.0))
            em("vector", lambda e: e.memset(consts[:, CEPSLN:CEPSLN + 1], LN_EPS))
            em("vector", lambda e: e.memset(ones1[:, :], 1.0))
            em("vector", lambda e: e.memset(ones128[:, :], 1.0))
            em("vector", lambda e: e.memset(hb[:, :], 0.0))
            em("vector", lambda e: e.memset(hb[0:64, 0:1], 0.5))
            em("vector", lambda e: e.memset(hb[64:128, 1:2], 0.5))
            em("vector", lambda e: e.memset(hmax[:, :], 0.0))
            em("vector", lambda e: e.memset(bdpm[:, :], 0.0))
            ds_p0.wait_all(em, "vector")
            em("vector", lambda e: e.tensor_scalar_mul(qkbs[:, :], qkbc[:, 0:16], 0.125))
            em("vector", lambda e: e.tensor_copy(pm2b[:, :], pm2[:, :]))
            vsp()
            em("vector", lambda e: e.tensor_copy(bdpm[0:64, 0:64], pm2b[0:64, :]))
            v_setup = emi("vector", lambda e: e.tensor_copy(bdpm[64:128, 64:128], pm2b[64:128, :]), cv)

            ds_p0.wait_all(em, "tensor")
            wait("tensor", cv, v_setup)
            for half in range(2):
                em("tensor", lambda e, h=half: e.matmul(
                    ps_b1[:, :], ones1[:, :],
                    qb_row[0:1, 2 * D + 512 * h:2 * D + 512 * (h + 1)], start=True, stop=True))
                tb = emi("tensor", lambda e, h=half: e.matmul(
                    ps_b2[:, :], ones1[:, :], pb_row[0:1, 512 * h:512 * (h + 1)],
                    start=True, stop=True), ct)
                wait("vector", ct, tb)
                em("vector", lambda e, h=half: e.tensor_copy(vbias[:, 512 * h:512 * (h + 1)], ps_b1[:, :]))
                vb = emi("vector", lambda e, h=half: e.tensor_copy(
                    pbias[:, 512 * h:512 * (h + 1)], ps_b2[:, :]), cv)
                wait("tensor", cv, vb)

        if stop_after == "P0":
            return _finish(nc, block, ops, barrier, _cleanup)
        with ExitStack() as a0s:
            en = a0s.enter_context
            qkvwT_sb = en(nc.sbuf_tensor("qkvwT_sb", [P, DC, 3 * D], BF16))
            xn_fm = en(nc.sbuf_tensor("xn_fm", [P, DC, T], BF16))
            x_t = en(nc.sbuf_tensor("x_t", [P, 2, D], F32))
            xc_t = en(nc.sbuf_tensor("xc_t", [P, D], F32))
            sq_t = en(nc.sbuf_tensor("sq_t", [P, D], F32))
            xnf_t = en(nc.sbuf_tensor("xnf_t", [P, D], F32))
            u_t = en(nc.sbuf_tensor("u_t", [P, D], F32))
            qi_t = en(nc.sbuf_tensor("qi_t", [P, D], I32))
            c2_t = en(nc.sbuf_tensor("c2_t", [P, D], F32))
            ii_t = en(nc.sbuf_tensor("ii_t", [P, D], F32))
            xnb_t = en(nc.sbuf_tensor("xnb_t", [P, 2, D], BF16))
            xqb_t = en(nc.sbuf_tensor("xqb_t", [P, 2, D], BF16))
            stats = en(nc.sbuf_tensor("stats", [P, 16], F32))
            v_ext = en(nc.sbuf_tensor("v_ext", [P, NB, 8 * 129], BF16))
            vscr = en(nc.sbuf_tensor("vscr", [P, 2, 512], F32))
            dn_t = en(nc.sbuf_tensor("dn_t", [P, 2, 512], F32))
            dnsqb = en(nc.sbuf_tensor("dnsqb", [P, 2, 512], BF16))
            dnb = en(nc.sbuf_tensor("dnb", [P, 2, 512], BF16))
            comb2 = en(nc.sbuf_tensor("comb2", [P, 2, 512], F32))
            comb3 = en(nc.sbuf_tensor("comb3", [P, 2, 512], F32))
            ex_t = en(nc.sbuf_tensor("ex_t", [P, 2, 512], F32))
            kpq = en(nc.sbuf_tensor("kpq", [P, 2, 512], BF16))
            ns_sb = en(nc.sbuf_tensor("ns_sb", [P, 2, 8], F32))
            rm8 = en(nc.sbuf_tensor("rm8", [P, 2, 8], F32))
            rm2 = en(nc.sbuf_tensor("rm2", [P, 2, 8], F32))
            qp_blk = en(nc.sbuf_tensor("qp_blk", [P, 2, NB, D], BF16))

            pp = [en(nc.psum_tensor(f"pp{i}", [P, 512], F32)) for i in range(2)]
            pc = [en(nc.psum_tensor(f"pc{i}", [P, 512], F32)) for i in range(2)]
            pns = en(nc.psum_tensor("pns", [P, 512], F32))
            pkv = [en(nc.psum_tensor(f"pkv{i}", [P, 512], F32)) for i in range(3)]

            qkvwT_v = dt_qkvwT.rearrange("(a p) c -> p a c", p=P)
            wait("gpsimd", cv, vb)
            wait("sync", cv, vb)
            for dc in range(DC):
                ds_qkvw.issue(em, "gpsimd", lambda e, d=dc: e.dma_start(
                    qkvwT_sb[:, d, :], qkvwT_v[:, d, :]))

            xn_store = {}
            xq_store = {}
            ln_done = [0, 0]
            var_done = [0, 0]
            ppfree = [0, 0]
            ppfill = [0, 0]
            pcfree = [0, 0]
            vext_ok = [0] * NB
            qp_store = [None, None]
            ns_copied = 0
            last_kv = 0
            kv_blk_done = 0
            kpq_free = [0, 0]
            dn_free = [0, 0]
            pi = ci = di = 0

            for b in range(NB):
                for tl in range(NB):
                    tt = NB * b + tl
                    xi = tt % 2
                    sg0 = 10 + 2 * xi
                    wait("sync", cv, ln_done[xi])
                    x_tok = ds_x.issue(em, "sync", lambda e, t=tt, i=xi: e.dma_start(
                        x_t[:, i, :], dt_x[P * t:P * (t + 1), :]), tt)
                    ds_x.wait(em, "scalar", x_tok)
                    wait("scalar", cv, var_done[xi])
                    em("scalar", lambda e, i=xi, s=sg0: e.activation(
                        xc_t[:, :], x_t[:, i, :], AF.Copy, accum_out=stats[:, s:s + 1]))
                    a1 = emi("scalar", lambda e, i=xi, s=sg0: e.activation(
                        sq_t[:, :], x_t[:, i, :], AF.Square,
                        accum_out=stats[:, s + 1:s + 2]), ca)
                    wait("vector", ca, a1)
                    em("vector", lambda e, s=sg0: e.tensor_scalar_mul(
                        stats[:, 1:2], stats[:, s:s + 1], 1.0 / D))
                    vsp()
                    em("vector", lambda e: e.tensor_tensor(
                        stats[:, 2:3], stats[:, 1:2], stats[:, 1:2], ALU.mult))
                    vsp()
                    v2 = emi("vector", lambda e, s=sg0: e.tensor_scalar(
                        stats[:, 3:4], stats[:, s + 1:s + 2], 1.0 / D, stats[:, 2:3],
                        ALU.mult, ALU.subtract), cv)
                    var_done[xi] = v2
                    wait("scalar", cv, v2)
                    a2 = emi("scalar", lambda e: e.activation(
                        stats[:, 4:5], stats[:, 3:4], AF.Sqrt,
                        bias=consts[:, CEPSLN:CEPSLN + 1]), ca)
                    wait("vector", ca, a2)
                    em("vector", lambda e: e.reciprocal(stats[:, 5:6], stats[:, 4:5]))
                    vsp()
                    em("vector", lambda e, i=xi: e.tensor_scalar(
                        xnf_t[:, :], x_t[:, i, :], stats[:, 1:2], stats[:, 5:6],
                        ALU.subtract, ALU.mult))
                    ds_xn.wait(em, "vector", xn_store.get(tt - 2))
                    v3 = emi("vector", lambda e, i=xi: e.tensor_copy(xnb_t[:, i, :], xnf_t[:, :]), cv)
                    wait("sync", cv, v3)
                    xn_store[tt] = ds_xn.issue(em, "sync", lambda e, t=tt, i=xi: e.dma_start(
                        d_xn[P * t:P * (t + 1), :], xnb_t[:, i, :]), tt)
                    em("vector", lambda e: e.tensor_reduce(
                        stats[:, 6:7], xnf_t[:, :], AX.X, ALU.max,
                        apply_absolute_value=True))
                    vsp()
                    em("vector", lambda e: e.tensor_scalar_max(stats[:, 7:8], stats[:, 6:7], 1e-5))
                    vsp()
                    em("vector", lambda e: e.reciprocal(stats[:, 8:9], stats[:, 7:8]))
                    vsp()
                    em("vector", lambda e: e.tensor_scalar_mul(stats[:, 9:10], stats[:, 8:9], 127.0))
                    vsp()
                    em("vector", lambda e, t=tt: e.reciprocal(rs_a[:, t:t + 1], stats[:, 9:10]))
                    em("vector", lambda e: e.tensor_scalar(
                        u_t[:, :], xnf_t[:, :], stats[:, 9:10], 256.0, ALU.mult, ALU.add))
                    em("vector", lambda e: e.tensor_copy(qi_t[:, :], u_t[:, :]))
                    em("vector", lambda e: e.tensor_scalar_add(c2_t[:, :], qi_t[:, :], -256.0))
                    em("vector", lambda e: e.scalar_tensor_tensor(
                        ii_t[:, :], u_t[:, :], 256.5, c2_t[:, :], ALU.subtract, ALU.is_ge))
                    ds_xq.wait(em, "vector", xq_store.get(tt - 2))
                    ln_done[xi] = emi("vector", lambda e, i=xi: e.tensor_tensor(
                        xqb_t[:, i, :], c2_t[:, :], ii_t[:, :], ALU.add), cv)
                    wait("sync", cv, ln_done[xi])
                    xq_store[tt] = ds_xq.issue(em, "sync", lambda e, t=tt, i=xi: e.dma_start(
                        d_xq[P * t:P * (t + 1), :], xqb_t[:, i, :]), tt)

                if stop_after == f"LN{b}":
                    return _finish(nc, block, ops, barrier, _cleanup)
                ds_xn.wait_all(em, "sync")
                for dc in range(DC):
                    tr_tok = ds_tr.issue(em, "sync", lambda e, d=dc, bb=b: e.dma_start(
                        xn_fm[:, d, 512 * bb:512 * (bb + 1)],
                        d_xn[512 * bb:512 * (bb + 1), P * d:P * (d + 1)],
                        transpose=True), b)

                if stop_after == f"TR{b}":
                    return _finish(nc, block, ops, barrier, _cleanup)
                ds_tr.wait(em, "tensor", tr_tok)
                if b == 0:
                    ds_qkvw.wait_all(em, "tensor")
                wait("vector", ct, kv_blk_done)
                if b == 0:
                    em("vector", lambda e: e.memset(
                        v_ext[:, :, :].rearrange("p b (a c) -> p b a c", c=129)[:, :, :, 128:129], 1.0))
                if stop_after == f"VS{b}":
                    return _finish(nc, block, ops, barrier, _cleanup)
                for tl in range(NB):
                    for vo in range(2):
                        i = pi; pi ^= 1
                        wait("tensor", cv, ppfree[i])
                        for dc in range(DC - 1):
                            em("tensor", lambda e, d=dc, bb=b, t=tl, v=vo, i_=i: e.matmul(
                                pp[i_][:, :],
                                xn_fm[:, d, 512 * bb + P * t:512 * bb + P * (t + 1)],
                                qkvwT_sb[:, d, 2 * D + 512 * v:2 * D + 512 * (v + 1)],
                                start=(d == 0), stop=False))
                        ppfill[i] = emi("tensor", lambda e, d=DC - 1, bb=b, t=tl, v=vo, i_=i: e.matmul(
                            pp[i_][:, :],
                            xn_fm[:, d, 512 * bb + P * t:512 * bb + P * (t + 1)],
                            qkvwT_sb[:, d, 2 * D + 512 * v:2 * D + 512 * (v + 1)],
                            start=False, stop=True), ct)
                        if stop_after == f"VM{b}{tl}{vo}":
                            return _finish(nc, block, ops, barrier, _cleanup)
                        wait("vector", ct, ppfill[i])
                        if stop_after == f"VW{b}{tl}{vo}":
                            return _finish(nc, block, ops, barrier, _cleanup)
                        em("vector", lambda e, v=vo, i_=i: e.tensor_tensor(
                            vscr[:, i_, :], pp[i_][:, :], vbias[:, 512 * v:512 * (v + 1)], ALU.add))
                        if stop_after == f"VT{b}{tl}{vo}":
                            return _finish(nc, block, ops, barrier, _cleanup)
                        ppfree[i] = emi("vector", lambda e, t=tl, v=vo, i_=i: e.tensor_copy(
                            v_ext[:, t, :].rearrange("p (a c) -> p a c", c=129)[:, 4 * v:4 * (v + 1), 0:128],
                            vscr[:, i_, :].rearrange("p (a c) -> p a c", c=128)), cv)
                        if stop_after == f"VC{b}{tl}{vo}":
                            return _finish(nc, block, ops, barrier, _cleanup)
                vext_ok[b] = ppfree[i]

                if stop_after == f"V{b}":
                    return _finish(nc, block, ops, barrier, _cleanup)
                for oc in range(16):
                    ocg = 16 * b + oc
                    i = pi; pi ^= 1
                    wait("tensor", cv, ppfree[i])
                    for dc in range(DC - 1):
                        em("tensor", lambda e, d=dc, o=oc, bb=b, i_=i: e.matmul(
                            pp[i_][:, :],
                            qkvwT_sb[:, d, P * o:P * (o + 1)],
                            xn_fm[:, d, 512 * bb:512 * (bb + 1)],
                            start=(d == 0), stop=False))
                    ppfill[i] = emi("tensor", lambda e, d=DC - 1, o=oc, bb=b, i_=i: e.matmul(
                        pp[i_][:, :],
                        qkvwT_sb[:, d, P * o:P * (o + 1)],
                        xn_fm[:, d, 512 * bb:512 * (bb + 1)],
                        start=False, stop=True), ct)

                    dd = di; di ^= 1
                    wait("vector", ct, max(ppfill[i], dn_free[dd]))
                    em("vector", lambda e, o=oc, i_=i, d=dd: e.tensor_scalar(
                        dn_t[:, d, :], pp[i_][:, :], 0.125, qkbs[:, o:o + 1], ALU.mult, ALU.add))
                    ppfree[i] = emi("vector", lambda e, d=dd: e.tensor_copy(
                        dnb[:, d, :], dn_t[:, d, :]), cv)
                    wait("scalar", cv, ppfree[i])
                    a_sq = emi("scalar", lambda e, d=dd: e.activation(
                        dnsqb[:, d, :], dn_t[:, d, :], AF.Square), ca)

                    cci = ci; ci ^= 1
                    nscol = (ocg % 64) * 8
                    wait("tensor", ca, a_sq)
                    wait("tensor", cv, max(pcfree[cci], ns_copied))
                    for tl in range(NB):
                        em("tensor", lambda e, t=tl, d=dd, c=cci: e.matmul(
                            pc[c][:, P * t:P * (t + 1)], dnb[:, d, P * t:P * (t + 1)],
                            bdpm[:, :], start=True, stop=True))
                    for tl in range(NB - 1):
                        em("tensor", lambda e, t=tl, d=dd, nco=nscol: e.matmul(
                            pns[:, nco + 2 * t:nco + 2 * (t + 1)],
                            dnsqb[:, d, P * t:P * (t + 1)], hb[:, :], start=True, stop=True))
                    pcfill = emi("tensor", lambda e, t=NB - 1, d=dd, nco=nscol: e.matmul(
                        pns[:, nco + 2 * t:nco + 2 * (t + 1)],
                        dnsqb[:, d, P * t:P * (t + 1)], hb[:, :], start=True, stop=True), ct)
                    dn_free[dd] = pcfill

                    wait("vector", ct, pcfill)
                    ns_copied = emi("vector", lambda e, d=dd, nco=nscol: e.tensor_copy(
                        ns_sb[:, d, :], pns[:, nco:nco + 8]), cv)
                    em("vector", lambda e, d=dd, c=cci: e.reduce_max(
                        rm8[:, d, :],
                        pc[c][:, :].rearrange("p (a c) -> p a c", c=64), AX.X))
                    for tl in range(NB):
                        for h2 in range(2):
                            if tl == NB - 1 and h2 == 1:
                                pcfree[cci] = emi("vector", lambda e, t=tl, h=h2, d=dd, c=cci: e.tensor_single_scalar(
                                    comb2[:, d, P * t + 64 * h:P * t + 64 * (h + 1)],
                                    pc[c][:, P * t + 64 * h:P * t + 64 * (h + 1)],
                                    ns_sb[:, d, 2 * t + h:2 * t + h + 1], ALU.subtract), cv)
                            else:
                                em("vector", lambda e, t=tl, h=h2, d=dd, c=cci: e.tensor_single_scalar(
                                    comb2[:, d, P * t + 64 * h:P * t + 64 * (h + 1)],
                                    pc[c][:, P * t + 64 * h:P * t + 64 * (h + 1)],
                                    ns_sb[:, d, 2 * t + h:2 * t + h + 1], ALU.subtract))
                    em("vector", lambda e, d=dd: e.tensor_tensor(
                        rm2[:, d, :], rm8[:, d, :], ns_sb[:, d, :], ALU.subtract))
                    vsp()
                    for tl in range(NB):
                        for h2 in range(2):
                            if tl == NB - 1 and h2 == 1:
                                v_c3 = emi("vector", lambda e, t=tl, h=h2, d=dd: e.tensor_single_scalar(
                                    comb3[:, d, P * t + 64 * h:P * t + 64 * (h + 1)],
                                    comb2[:, d, P * t + 64 * h:P * t + 64 * (h + 1)],
                                    rm2[:, d, 2 * t + h:2 * t + h + 1], ALU.subtract), cv)
                            else:
                                em("vector", lambda e, t=tl, h=h2, d=dd: e.tensor_single_scalar(
                                    comb3[:, d, P * t + 64 * h:P * t + 64 * (h + 1)],
                                    comb2[:, d, P * t + 64 * h:P * t + 64 * (h + 1)],
                                    rm2[:, d, 2 * t + h:2 * t + h + 1], ALU.subtract))
                    wait("scalar", cv, v_c3)
                    a_ex = emi("scalar", lambda e, d=dd: e.activation(
                        ex_t[:, d, :], comb3[:, d, :], AF.Exp), ca)
                    wait("vector", ca, a_ex)
                    if oc < 8:
                        if b >= 2 and oc == 0:
                            ds_qp.wait(em, "vector", qp_store[b % 2])
                        v_qp = emi("vector", lambda e, d=dd, bb=b, o=oc: e.tensor_scalar(
                            qp_blk[:, bb % 2, :, P * o:P * (o + 1)],
                            ex_t[:, d, :].rearrange("p (a c) -> p a c", c=P),
                            0.125, EPS, ALU.mult, ALU.add), cv)
                        if oc == 7:
                            wait("sync", cv, v_qp)
                            qp_store[b % 2] = ds_qp.issue(em, "sync", lambda e, bb=b: e.dma_start(
                                d_qp[512 * bb:512 * (bb + 1), :].rearrange("(a p) c -> p a c", p=P),
                                qp_blk[:, bb % 2, :, :]), b)
                        if stop_after == f"QP{b}{oc}":
                            return _finish(nc, block, ops, barrier, _cleanup)
                    else:
                        pr = oc - 8
                        kk = dd
                        wait("vector", ct, kpq_free[kk])
                        v_kp = emi("vector", lambda e, d=dd, k=kk: e.tensor_scalar(
                            kpq[:, k, :], ex_t[:, d, :], 0.125, EPS, ALU.mult, ALU.add), cv)
                        wait("tensor", cv, max(v_kp, vext_ok[b]))
                        def kvmm(e, t, k, p_, bb):
                            return e.matmul(
                                pkv[p_ // 3][:, 129 * (p_ % 3):129 * (p_ % 3) + 129],
                                kpq[:, k, P * t:P * (t + 1)],
                                v_ext[:, t, 129 * p_:129 * (p_ + 1)],
                                start=(bb == 0 and t == 0 and p_ % 3 == 0),
                                stop=(bb == NB - 1 and t == NB - 1 and (p_ % 3 == 2 or p_ == 7)))
                        for tl in range(NB - 1):
                            em("tensor", lambda e, t=tl, k=kk, p_=pr, bb=b, f=kvmm: f(e, t, k, p_, bb))
                        kpq_free[kk] = emi("tensor", lambda e, t=NB - 1, k=kk, p_=pr, bb=b, f=kvmm: f(e, t, k, p_, bb), ct)
                        if oc == 15:
                            kv_blk_done = kpq_free[kk]
                        if b == NB - 1 and oc == 15:
                            last_kv = kpq_free[kk]

            if stop_after == "A0":
                return _finish(nc, block, ops, barrier, _cleanup)
            w1_t = en(nc.sbuf_tensor("w1_t", [P, 2, 512], F32))
            w1part = en(nc.sbuf_tensor("w1part", [P, 8], F32))
            w1_done = [0, 0]
            w1srcs = (
                [dt_wgm[:, 512 * j:512 * (j + 1)] for j in range(8)],
                [dt_wum[:, 512 * j:512 * (j + 1)] for j in range(8)],
                [dt_wdm[P * (j // 2):P * (j // 2 + 1), 512 * (j % 2):512 * (j % 2 + 1)]
                 for j in range(8)],
            )
            for mi_ in range(3):
                for j in range(8):
                    i = j % 2
                    wait("sync", cv, w1_done[i])
                    w1_tok = ds_wt.issue(em, "sync", lambda e, s=w1srcs[mi_][j], i_=i: e.dma_start(
                        w1_t[:, i_, :], s), j)
                    ds_wt.wait(em, "vector", w1_tok)
                    w1_done[i] = emi("vector", lambda e, j_=j, i_=i: e.tensor_reduce(
                        w1part[:, j_:j_ + 1], w1_t[:, i_, :], AX.X, ALU.add,
                        apply_absolute_value=True), cv)
                vsp()
                em("vector", lambda e, m=mi_: e.reduce_sum(
                    cc_sb[:, 1032 + m:1033 + m], w1part[:, 0:8], AX.X))
            wait("vector", ct, last_kv)
            em("vector", lambda e: e.tensor_copy(cc_sb[:, 0:387], pkv[0][:, 0:387]))
            em("vector", lambda e: e.tensor_copy(cc_sb[:, 387:774], pkv[1][:, 0:387]))
            v_cc = emi("vector", lambda e: e.tensor_copy(cc_sb[:, 774:1032], pkv[2][:, 0:258]), cv)
            wait("gpsimd", cv, v_cc)
            cc_tok = ds_cc.issue(em, "gpsimd", lambda e: e.dma_start(d_ccin[:, :], cc_sb[:, :]))
            ds_cc.wait(em, "gpsimd", cc_tok)
            emi("gpsimd", lambda e: e.collective_compute(
                "AllReduce", ALU.add, replica_groups=[list(range(NCORES))],
                ins=[d_ccin.opt()], outs=[d_ccout.opt()]), ccs, 1)
            wait("gpsimd", ccs, ccs.n)
            g_ccback = ds_cc.issue(em, "gpsimd", lambda e: e.dma_start(cc_sb[:, :], d_ccout[:, :]))
            wait("sync", ct, last_kv)
            ds_qp.wait_all(em, "sync")
            ds_xn.wait_all(em, "sync")
            ds_xq.wait_all(em, "sync")

        if stop_after == "AR":
            return _finish(nc, block, ops, barrier, _cleanup)
        with ExitStack() as ws:
            en = ws.enter_context
            w_t = en(nc.sbuf_tensor("w_t", [P, 2, INNER], F32))
            v_s = en(nc.sbuf_tensor("v_s", [P, 2, INNER], F32))
            i1_s = en(nc.sbuf_tensor("i1_s", [P, 2, INNER], F32))
            i2_s = en(nc.sbuf_tensor("i2_s", [P, 2, INNER], F32))
            tern = en(nc.sbuf_tensor("tern", [P, 2, INNER], BF16))
            ps_s = en(nc.psum_tensor("ps_s", [P, 8], F32))
            ps_bb = en(nc.psum_tensor("ps_bb", [P, 8], F32))

            mats = [(dt_wgm, d_wqg_m, 1, INNER, 0), (dt_wum, d_wqu_m, 1, INNER, 1),
                    (dt_wdm, d_wqd_m, 4, D, 2)]
            red_done = [0, 0]
            tern_done = [0, 0]
            tn_used = [0, 0]
            tern_stored = [None, None]
            i1_done = [0, 0]
            i2_done = [0, 0]
            nel = float(D * INNER)
            ds_cc.wait(em, "tensor", g_ccback)
            for (src, dst, ntile, C, mi) in mats:
                t_ws = emi("tensor", lambda e, m=mi: e.matmul(
                    ps_s[0:1, m:m + 1], cc_sb[:, 1032 + m:1033 + m], ones128[:, :],
                    start=True, stop=True), ct)
                wait("vector", ct, t_ws)
                em("vector", lambda e, m=mi, nn=nel: e.tensor_scalar_mul(
                    ws11[0:1, 3:4], ps_s[0:1, m:m + 1], 1.0 / nn))
                vsp()
                em("vector", lambda e: e.tensor_scalar_max(ws11[0:1, 3:4], ws11[0:1, 3:4], 1e-5))
                vsp()
                v_s11 = emi("vector", lambda e, m=mi: e.reciprocal(
                    ws11[0:1, m:m + 1], ws11[0:1, 3:4]), cv)
                wait("tensor", cv, v_s11)
                t_bb = emi("tensor", lambda e, m=mi: e.matmul(
                    ps_bb[:, m:m + 1], ones1[:, :], ws11[0:1, m:m + 1],
                    start=True, stop=True), ct)
                wait("vector", ct, t_bb)
                em("vector", lambda e, m=mi: e.tensor_copy(ws_bc[:, m:m + 1], ps_bb[:, m:m + 1]))
                vsp()
                v_wsb = emi("vector", lambda e, m=mi: e.reciprocal(
                    rw_bc[:, m:m + 1], ws_bc[:, m:m + 1]), cv)

                for it in range(ntile):
                    i = it % 2
                    wait("sync", ca, tern_done[i])
                    wait("sync", cv, red_done[i])
                    wt_tok = ds_wt.issue(em, "sync", lambda e, s=src, t=it, i_=i, c=C: e.dma_start(
                        w_t[:, i_, 0:c], s[P * t:P * (t + 1), :]), it)
                    ds_wt.wait(em, "scalar", wt_tok)
                    wait("scalar", cv, max(v_wsb, i1_done[i]))
                    wait("scalar", cg, i2_done[i])
                    tern_done[i] = emi("scalar", lambda e, i_=i, c=C, m=mi: e.activation(
                        v_s[:, i_, 0:c], w_t[:, i_, 0:c], AF.Copy,
                        scale=ws_bc[:, m:m + 1]), ca)
                    wait("vector", ca, tern_done[i])
                    i1_done[i] = emi("vector", lambda e, i_=i, c=C: e.tensor_single_scalar(
                        i1_s[:, i_, 0:c], v_s[:, i_, 0:c], 0.5, ALU.is_ge), cv)
                    vsp()
                    wait("gpsimd", ca, tern_done[i])
                    wait("gpsimd", cv, tn_used[i])
                    g_i2 = emi("gpsimd", lambda e, i_=i, c=C: e.tensor_single_scalar(
                        i2_s[:, i_, 0:c], v_s[:, i_, 0:c], -0.5, ALU.is_le), cg)
                    i2_done[i] = g_i2
                    wait("vector", cg, g_i2)
                    ds_wq.wait(em, "vector", tern_stored[i])
                    v_tn = emi("vector", lambda e, i_=i, c=C: e.tensor_tensor(
                        tern[:, i_, 0:c], i1_s[:, i_, 0:c], i2_s[:, i_, 0:c], ALU.subtract), cv)
                    tn_used[i] = v_tn
                    wait("gpsimd", cv, v_tn)
                    tern_stored[i] = ds_wq.issue(em, "gpsimd", lambda e, d=dst, t=it, i_=i, c=C: e.dma_start(
                        d[P * t:P * (t + 1), :], tern[:, i_, 0:c]), it)
            ds_wq.wait_all(em, "gpsimd")
            for (mine, full) in ((d_wqg_m, d_wqg), (d_wqu_m, d_wqu), (d_wqd_m, d_wqd)):
                emi("gpsimd", lambda e, a=mine, o=full: e.collective_compute(
                    "AllGather", ALU.bypass, replica_groups=[list(range(NCORES))],
                    ins=[a.opt()], outs=[o.opt()]), ccs, 1)
            ccs_gathers = ccs.n
            v_wsb_last = v_wsb
            if debug:
                wait("gpsimd", ccs, ccs_gathers)
                ds_wq.issue(em, "gpsimd", lambda e: e.dma_start(
                    d_wqgdump[:, :], d_wqg[:, :]))
            if stop_after == "W":
                barrier()

        if stop_after == "W":
            return _finish(nc, block, ops, barrier, _cleanup)
        with ExitStack() as a1s:
            en = a1s.enter_context
            xq_fm = en(nc.sbuf_tensor("xq_fm", [P, DC, T], BF16))
            wgst = en(nc.sbuf_tensor("wgst", [P, 2, DC, 512], BF16))
            wust = en(nc.sbuf_tensor("wust", [P, 2, DC, 512], BF16))
            gate_f = en(nc.sbuf_tensor("gate_f", [P, 2, 512], F32))
            up_f = en(nc.sbuf_tensor("up_f", [P, 2, 512], F32))
            h_f = en(nc.sbuf_tensor("h_f", [P, 2, 512], F32))
            hred = en(nc.sbuf_tensor("hred", [P, 1], F32))
            psG = [en(nc.psum_tensor(f"psG{i}", [P, 512], F32)) for i in range(2)]
            psU = [en(nc.psum_tensor(f"psU{i}", [P, 512], F32)) for i in range(2)]

            ds_wq.wait_all(em, "sync")
            for dc in range(DC):
                ds_fm1.issue(em, "sync", lambda e, d=dc: e.dma_start(
                    xq_fm[:, d, :], d_xq[:, P * d:P * (d + 1)], transpose=True))
            wait("sync", ccs, ccs_gathers)
            em("vector", lambda e: e.tensor_single_scalar(cgate[:, :], rs_a[:, :], rw_bc[:, 0:1], ALU.mult))
            v_cg = emi("vector", lambda e: e.tensor_single_scalar(
                cup[:, :], rs_a[:, :], rw_bc[:, 1:2], ALU.mult), cv)

            wq_g_v = d_wqg.rearrange("(a p) c -> p a c", p=P)
            wq_u_v = d_wqu.rearrange("(a p) c -> p a c", p=P)
            stage_free = [0, 0]
            psfree = [0, 0]
            hstore = [None, None]
            hfree = [0, 0]
            for og in range(NOG):
                i = og % 2
                wait("sync", ct, stage_free[i])
                ds_st.issue(em, "sync", lambda e, o=og, i_=i: e.dma_start(
                    wgst[:, i_, :, :], wq_g_v[:, :, 512 * o:512 * (o + 1)]), og)
                st_tok = ds_st.issue(em, "sync", lambda e, o=og, i_=i: e.dma_start(
                    wust[:, i_, :, :], wq_u_v[:, :, 512 * o:512 * (o + 1)]), og)
                for tt in range(NT):
                    j = tt % 2
                    if og == 0 and tt == 0:
                        ds_fm1.wait_all(em, "tensor")
                        wait("tensor", cv, v_wsb_last)
                    ds_st.wait(em, "tensor", st_tok)
                    wait("tensor", ca, psfree[j])
                    for dc in range(DC):
                        em("tensor", lambda e, d=dc, t=tt, i_=i, j_=j: e.matmul(
                            psG[j_][:, :], xq_fm[:, d, P * t:P * (t + 1)],
                            wgst[:, i_, d, :], start=(d == 0), stop=(d == DC - 1)))
                    for dc in range(DC - 1):
                        em("tensor", lambda e, d=dc, t=tt, i_=i, j_=j: e.matmul(
                            psU[j_][:, :], xq_fm[:, d, P * t:P * (t + 1)],
                            wust[:, i_, d, :], start=(d == 0), stop=False))
                    t_gu = emi("tensor", lambda e, d=DC - 1, t=tt, i_=i, j_=j: e.matmul(
                        psU[j_][:, :], xq_fm[:, d, P * t:P * (t + 1)],
                        wust[:, i_, d, :], start=False, stop=True), ct)
                    if tt == NT - 1:
                        stage_free[i] = t_gu
                    wait("scalar", ct, t_gu)
                    wait("scalar", cv, max(v_cg, hfree[j]))
                    em("scalar", lambda e, t=tt, j_=j: e.activation(
                        gate_f[:, j_, :], psG[j_][:, :], AF.Silu, scale=cgate[:, t:t + 1]))
                    psfree[j] = emi("scalar", lambda e, t=tt, j_=j: e.activation(
                        up_f[:, j_, :], psU[j_][:, :], AF.Copy, scale=cup[:, t:t + 1]), ca)
                    wait("vector", ca, psfree[j])
                    ds_h.wait(em, "vector", hstore[j])
                    em("vector", lambda e, j_=j: e.tensor_tensor(
                        h_f[:, j_, :], gate_f[:, j_, :], up_f[:, j_, :], ALU.mult))
                    em("vector", lambda e, j_=j: e.tensor_reduce(
                        hred[:, :], h_f[:, j_, :], AX.X, ALU.max, apply_absolute_value=True))
                    vsp()
                    hfree[j] = emi("vector", lambda e, t=tt: e.tensor_tensor(
                        hmax[:, t:t + 1], hmax[:, t:t + 1], hred[:, :], ALU.max), cv)
                    wait("sync", cv, hfree[j])
                    hstore[j] = ds_h.issue(em, "sync", lambda e, t=tt, o=og, j_=j: e.dma_start(
                        d_h[P * t:P * (t + 1), 512 * o:512 * (o + 1)], h_f[:, j_, :]), tt)
            vsp()
            em("vector", lambda e: e.tensor_scalar_max(scr16[:, :], hmax[:, :], 1e-5))
            vsp()
            em("vector", lambda e: e.reciprocal(scr16[:, :], scr16[:, :]))
            vsp()
            em("vector", lambda e: e.tensor_scalar_mul(sh_all[:, :], scr16[:, :], 127.0))
            vsp()
            em("vector", lambda e: e.reciprocal(scr16[:, :], sh_all[:, :]))
            vsp()
            emi("vector", lambda e: e.tensor_single_scalar(
                cdown[:, :], scr16[:, :], rw_bc[:, 2:3], ALU.mult), cv)
            barrier()

        if stop_after == "A1":
            return _finish(nc, block, ops, barrier, _cleanup)
        with ExitStack() as a2s:
            en = a2s.enter_context
            HW = 2048
            h_t = en(nc.sbuf_tensor("h_t", [P, 2, HW], F32))
            u2 = en(nc.sbuf_tensor("u2", [P, 2, HW], F32))
            qi2 = en(nc.sbuf_tensor("qi2", [P, 2, HW], I32))
            c22 = en(nc.sbuf_tensor("c22", [P, 2, HW], F32))
            ii2 = en(nc.sbuf_tensor("ii2", [P, 2, HW], F32))
            hqb = en(nc.sbuf_tensor("hqb", [P, 2, HW], BF16))
            qp_fm = en(nc.sbuf_tensor("qp_fm", [P, DC, T], BF16))
            a_blk = en(nc.sbuf_tensor("a_blk", [P, 2, D], BF16))
            nrm16 = en(nc.sbuf_tensor("nrm16", [P, 16], F32))
            nrm2 = en(nc.sbuf_tensor("nrm2", [P, 16], F32))
            rn16 = en(nc.sbuf_tensor("rn16", [P, 16], F32))
            psO = [en(nc.psum_tensor(f"psO{i}", [P, 512], F32)) for i in range(6)]

            done_i = [None, None]
            hq_done = [0, 0]

            def a2_iter(it):
                tt, hh = divmod(it, 2)
                i = it % 2
                ds_hq.wait(em, "sync", done_i[i])
                h2_tok = ds_h2.issue(em, "sync", lambda e, t=tt, h_=hh, i_=i: e.dma_start(
                    h_t[:, i_, :], d_h[P * t:P * (t + 1), HW * h_:HW * (h_ + 1)]), it)
                ds_h2.wait(em, "scalar", h2_tok)
                wait("scalar", cg, hq_done[i])
                a_u = emi("scalar", lambda e, t=tt, i_=i: e.activation(
                    u2[:, i_, :], h_t[:, i_, :], AF.Copy, bias=256.0,
                    scale=sh_all[:, t:t + 1]), ca)
                wait("vector", ca, a_u)
                wait("vector", cg, hq_done[i])
                em("vector", lambda e, i_=i: e.tensor_copy(qi2[:, i_, :], u2[:, i_, :]))
                em("vector", lambda e, i_=i: e.tensor_scalar_add(c22[:, i_, :], qi2[:, i_, :], -256.0))
                v_ii = emi("vector", lambda e, i_=i: e.scalar_tensor_tensor(
                    ii2[:, i_, :], u2[:, i_, :], 256.5, c22[:, i_, :],
                    ALU.subtract, ALU.is_ge), cv)
                wait("gpsimd", cv, v_ii)
                hq_done[i] = emi("gpsimd", lambda e, i_=i: e.tensor_tensor(
                    hqb[:, i_, :], c22[:, i_, :], ii2[:, i_, :], ALU.add), cg)
                wait("sync", cg, hq_done[i])
                done_i[i] = ds_hq.issue(em, "sync", lambda e, t=tt, h_=hh, i_=i: e.dma_start(
                    d_hq[P * t:P * (t + 1), HW * h_:HW * (h_ + 1)], hqb[:, i_, :]), it)

            ds_cc.wait(em, "vector", g_ccback)
            if debug:
                ds_cc.wait(em, "gpsimd", g_ccback)
                ds_cc.issue(em, "gpsimd", lambda e: e.dma_start(d_ccdump[:, :], cc_sb[:, :]))
            em("vector", lambda e: e.memset(kvr[:, :], 0.0))
            for p_ in range(8):
                base = 129 * p_
                ko = 130 * p_
                em("vector", lambda e, b_=base, k_=ko: e.tensor_copy(
                    kvr[0:64, k_:k_ + 64], cc_sb[0:64, b_:b_ + 64]))
                em("vector", lambda e, b_=base, k_=ko: e.tensor_copy(
                    kvr[0:64, k_ + 64:k_ + 65], cc_sb[0:64, b_ + 128:b_ + 129]))
                em("vector", lambda e, b_=base, k_=ko: e.tensor_copy(
                    kvr[64:128, k_ + 65:k_ + 129], cc_sb[64:128, b_ + 64:b_ + 128]))
                if p_ == 7:
                    v_kvr = emi("vector", lambda e, b_=base, k_=ko: e.tensor_copy(
                        kvr[64:128, k_ + 129:k_ + 130], cc_sb[64:128, b_ + 128:b_ + 129]), cv)
                else:
                    em("vector", lambda e, b_=base, k_=ko: e.tensor_copy(
                        kvr[64:128, k_ + 129:k_ + 130], cc_sb[64:128, b_ + 128:b_ + 129]))

            for dc in range(DC):
                ds_fm2.issue(em, "sync", lambda e, d=dc: e.dma_start(
                    qp_fm[:, d, :], d_qp[:, P * d:P * (d + 1)], transpose=True))

            oset_free = [0, 0]
            astore = [None, None]

            def b1_iter(tt):
                s = tt % 2
                if tt == 0:
                    ds_fm2.wait_all(em, "tensor")
                wait("tensor", cv, max(v_kvr, oset_free[s]))
                for p_ in range(8):
                    bank = 3 * s + p_ // 3
                    off = 130 * (p_ % 3)
                    if p_ == 7:
                        t_o = emi("tensor", lambda e, t=tt, p2=p_, b_=bank, o=off: e.matmul(
                            psO[b_][:, o:o + 130], qp_fm[:, p2, P * t:P * (t + 1)],
                            kvr[:, 130 * p2:130 * (p2 + 1)], start=True, stop=True), ct)
                    else:
                        em("tensor", lambda e, t=tt, p2=p_, b_=bank, o=off: e.matmul(
                            psO[b_][:, o:o + 130], qp_fm[:, p2, P * t:P * (t + 1)],
                            kvr[:, 130 * p2:130 * (p2 + 1)], start=True, stop=True))
                wait("vector", ct, t_o)
                ds_a.wait(em, "vector", astore[s])
                em("vector", lambda e, s_=s: e.tensor_copy(
                    nrm16[:, 0:6].rearrange("p (a b c) -> p a b c", a=3, b=2),
                    psO[3 * s_][:, 0:390].rearrange("p (a b c) -> p a b c", a=3, c=65)[:, :, :, 64:65]))
                em("vector", lambda e, s_=s: e.tensor_copy(
                    nrm16[:, 6:12].rearrange("p (a b c) -> p a b c", a=3, b=2),
                    psO[3 * s_ + 1][:, 0:390].rearrange("p (a b c) -> p a b c", a=3, c=65)[:, :, :, 64:65]))
                em("vector", lambda e, s_=s: e.tensor_copy(
                    nrm16[:, 12:16].rearrange("p (a b c) -> p a b c", a=2, b=2),
                    psO[3 * s_ + 2][:, 0:260].rearrange("p (a b c) -> p a b c", a=2, c=65)[:, :, :, 64:65]))
                vsp()
                em("vector", lambda e: e.tensor_scalar_add(nrm2[:, :], nrm16[:, :], EPS))
                vsp()
                em("vector", lambda e: e.reciprocal(rn16[:, :], nrm2[:, :]))
                vsp()
                for p_ in range(8):
                    bank = 3 * s + p_ // 3
                    off = 130 * (p_ % 3)
                    for h2 in range(2):
                        if p_ == 7 and h2 == 1:
                            oset_free[s] = emi("vector", lambda e, s_=s, p2=p_, b_=bank, o=off, h_=h2: e.tensor_single_scalar(
                                a_blk[:, s_, 64 * (2 * p2 + h_):64 * (2 * p2 + h_ + 1)],
                                psO[b_][:, o + 65 * h_:o + 65 * h_ + 64],
                                rn16[:, 2 * p2 + h_:2 * p2 + h_ + 1], ALU.mult), cv)
                        else:
                            em("vector", lambda e, s_=s, p2=p_, b_=bank, o=off, h_=h2: e.tensor_single_scalar(
                                a_blk[:, s_, 64 * (2 * p2 + h_):64 * (2 * p2 + h_ + 1)],
                                psO[b_][:, o + 65 * h_:o + 65 * h_ + 64],
                                rn16[:, 2 * p2 + h_:2 * p2 + h_ + 1], ALU.mult))
                wait("sync", cv, oset_free[s])
                astore[s] = ds_a.issue(em, "sync", lambda e, t=tt, s_=s: e.dma_start(
                    d_a[P * t:P * (t + 1), :], a_blk[:, s_, :]), tt)

            for tt in range(NT):
                a2_iter(2 * tt)
                a2_iter(2 * tt + 1)
                b1_iter(tt)
            barrier()

        if stop_after in ("A2", "B1"):
            return _finish(nc, block, ops, barrier, _cleanup)
        with ExitStack() as b2s:
            en = b2s.enter_context
            wdT_sb = en(nc.sbuf_tensor("wdT_sb", [P, NJC, D], BF16))
            projw_sb = en(nc.sbuf_tensor("projw_sb", [P, DC, D], BF16))
            a_fm = en(nc.sbuf_tensor("a_fm", [P, 2, DC, 512], BF16))
            hq_fm = en(nc.sbuf_tensor("hq_fm", [P, 2, NJC, 512], BF16))
            o1_s = en(nc.sbuf_tensor("o1_s", [P, 2, D], F32))
            t2_s = en(nc.sbuf_tensor("t2_s", [P, 2, D], F32))
            x2_t = en(nc.sbuf_tensor("x2_t", [P, 2, D], F32))
            psD = [en(nc.psum_tensor(f"psD{i}", [P, D], F32)) for i in range(2)]
            psP = [en(nc.psum_tensor(f"psP{i}", [P, D], F32)) for i in range(2)]

            wqd_v = d_wqd.rearrange("(a p) c -> p a c", p=P)
            for g4 in range(4):
                ds_wd.issue(em, "sync", lambda e, g_=g4: e.dma_start(
                    wdT_sb[:, 8 * g_:8 * (g_ + 1), :], wqd_v[:, 8 * g_:8 * (g_ + 1), :]))
            projw_v = dt_projwT.rearrange("(a p) c -> p a c", p=P)
            for dc in range(DC):
                ds_pj.issue(em, "gpsimd", lambda e, d=dc: e.dma_start(
                    projw_sb[:, d, :], projw_v[:, d, :]))

            b2_free = [0, 0]
            b2_tok = [None, None]
            psfree2 = [0, 0]
            psPfree = [0, 0]
            ostore = [None, None]

            def b2_loads(ss):
                sb = ss % 2
                wait("sync", ct, b2_free[sb])
                for jc in range(NJC):
                    ds_b2.issue(em, "sync", lambda e, ss_=ss, j=jc, s_=sb: e.dma_start(
                        hq_fm[:, s_, j, :],
                        d_hq[512 * ss_:512 * (ss_ + 1), P * j:P * (j + 1)],
                        transpose=True), ss)
                for dc in range(DC):
                    b2_tok[sb] = ds_b2.issue(em, "sync", lambda e, ss_=ss, d=dc, s_=sb: e.dma_start(
                        a_fm[:, s_, d, :],
                        d_a[512 * ss_:512 * (ss_ + 1), P * d:P * (d + 1)],
                        transpose=True), ss)

            def b2_compute(ss):
                sb = ss % 2
                for tl in range(NB):
                    tt = NB * ss + tl
                    j = tt % 2
                    wait("sync", cv, psPfree[j])
                    x2_tok = ds_x2.issue(em, "sync", lambda e, t=tt, j_=j: e.dma_start(
                        x2_t[:, j_, :], dt_x[P * t:P * (t + 1), :]), tt)
                    if ss == 0 and tl == 0:
                        ds_wd.wait_all(em, "tensor")
                        ds_pj.wait_all(em, "tensor")
                    ds_b2.wait(em, "tensor", b2_tok[sb])
                    wait("tensor", ca, psfree2[j])
                    wait("tensor", cv, psPfree[j])
                    for jc in range(NJC):
                        for hf in range(2):
                            em("tensor", lambda e, jj=jc, t=tl, j_=j, h_=hf, s_=sb: e.matmul(
                                psD[j_][:, 512 * h_:512 * (h_ + 1)],
                                hq_fm[:, s_, jj, P * t:P * (t + 1)],
                                wdT_sb[:, jj, 512 * h_:512 * (h_ + 1)],
                                start=(jj == 0), stop=(jj == NJC - 1)))
                    for dc in range(DC):
                        for hf in range(2):
                            if dc == DC - 1 and hf == 1:
                                t_b2 = emi("tensor", lambda e, d=dc, t=tl, j_=j, h_=hf, s_=sb: e.matmul(
                                    psP[j_][:, 512 * h_:512 * (h_ + 1)],
                                    a_fm[:, s_, d, P * t:P * (t + 1)],
                                    projw_sb[:, d, 512 * h_:512 * (h_ + 1)],
                                    start=(d == 0), stop=(d == DC - 1)), ct)
                            else:
                                em("tensor", lambda e, d=dc, t=tl, j_=j, h_=hf, s_=sb: e.matmul(
                                    psP[j_][:, 512 * h_:512 * (h_ + 1)],
                                    a_fm[:, s_, d, P * t:P * (t + 1)],
                                    projw_sb[:, d, 512 * h_:512 * (h_ + 1)],
                                    start=(d == 0), stop=(d == DC - 1)))
                    if tl == NB - 1:
                        b2_free[sb] = t_b2
                    wait("scalar", ct, t_b2)
                    wait("scalar", cv, psPfree[j])
                    psfree2[j] = emi("scalar", lambda e, t=tt, j_=j: e.activation(
                        o1_s[:, j_, :], psD[j_][:, :], AF.Copy, scale=cdown[:, t:t + 1]), ca)
                    wait("vector", ca, psfree2[j])
                    ds_x2.wait(em, "vector", x2_tok)
                    ds_out.wait(em, "vector", ostore[j])
                    em("vector", lambda e, j_=j: e.tensor_tensor(
                        t2_s[:, j_, :], o1_s[:, j_, :], psP[j_][:, :], ALU.add))
                    em("vector", lambda e, j_=j: e.tensor_tensor(
                        o1_s[:, j_, :], t2_s[:, j_, :], pbias[:, :], ALU.add))
                    psPfree[j] = emi("vector", lambda e, j_=j: e.tensor_tensor(
                        t2_s[:, j_, :], o1_s[:, j_, :], x2_t[:, j_, :], ALU.add), cv)
                    wait("sync", cv, psPfree[j])
                    ostore[j] = ds_out.issue(em, "sync", lambda e, t=tt, j_=j: e.dma_start(
                        dt_out[P * t:P * (t + 1), :], t2_s[:, j_, :]), tt)

            for ss in range(NB):
                b2_loads(ss)
                if ss >= 1:
                    b2_compute(ss - 1)
            b2_compute(NB - 1)
            ds_out.wait_all(em, "sync")
            barrier()

        return _finish(nc, block, ops, barrier, _cleanup)


_NC_CACHE = {}


def kernel(L0_attn_x, L0_attn_qkv_w, L0_attn_qkv_b, L0_attn_proj_w, L0_attn_proj_b,
           L0_attn_projection_matrix, L0_ffn_w_gate, L0_ffn_w_up, L0_ffn_w_down,
           debug=False):
    x = np.ascontiguousarray(np.asarray(L0_attn_x, dtype=np.float32).reshape(16384, D))
    qkv_w = np.asarray(L0_attn_qkv_w, dtype=np.float32)
    qkv_b = np.asarray(L0_attn_qkv_b, dtype=np.float32)
    proj_w = np.asarray(L0_attn_proj_w, dtype=np.float32)
    proj_b = np.asarray(L0_attn_proj_b, dtype=np.float32)
    pm = np.asarray(L0_attn_projection_matrix, dtype=np.float32)
    w_gate = np.asarray(L0_ffn_w_gate, dtype=np.float32)
    w_up = np.asarray(L0_ffn_w_up, dtype=np.float32)
    w_down = np.asarray(L0_ffn_w_down, dtype=np.float32)

    qkvwT = np.ascontiguousarray(qkv_w.T)
    qkbc = np.ascontiguousarray(qkv_b.reshape(24, P).T)
    projwT = np.ascontiguousarray(proj_w.T)
    pmT = np.ascontiguousarray(pm.T)
    wgT = np.ascontiguousarray(w_gate.T)
    wuT = np.ascontiguousarray(w_up.T)
    wdT = np.ascontiguousarray(w_down.T)
    qkvb2 = np.ascontiguousarray(qkv_b.reshape(1, 3 * D))
    projb2 = np.ascontiguousarray(proj_b.reshape(1, D))

    key = bool(debug)
    if key not in _NC_CACHE:
        _NC_CACHE[key] = build(debug=debug)
    nc = _NC_CACHE[key]

    in_maps = []
    for c in range(NCORES):
        in_maps.append({
            "x": np.ascontiguousarray(x[T * c:T * (c + 1), :]),
            "qkvwT": qkvwT, "qkv_b": qkvb2, "qkb_cols": qkbc,
            "projwT": projwT, "proj_b": projb2, "pmT": pmT,
            "wg_mine": np.ascontiguousarray(wgT[P * c:P * (c + 1), :]),
            "wu_mine": np.ascontiguousarray(wuT[P * c:P * (c + 1), :]),
            "wd_mine": np.ascontiguousarray(wdT[4 * P * c:4 * P * (c + 1), :]),
        })
    res = run_bass_kernel_spmd(nc, in_maps, core_ids=list(range(NCORES)))
    out = np.concatenate([np.asarray(r["out"]) for r in res.results], axis=0)
    if debug:
        return out.reshape(1, 16384, D), res
    return out.reshape(1, 16384, D)


# version 18
# speedup vs baseline: 1.2105x; 1.2105x over previous
from contextlib import ExitStack

import numpy as np

import concourse.bass as bass
import concourse.mybir as mybir
from concourse.bass_utils import run_bass_kernel_spmd

F32 = mybir.dt.float32
BF16 = mybir.dt.bfloat16
I32 = mybir.dt.int32
AF = mybir.ActivationFunctionType
ALU = mybir.AluOpType
AX = mybir.AxisListType

NCORES = 8
T = 2048
P = 128
NT = T // P
NB = 4
D = 1024
DC = D // P
HD = 64
INNER = 4096
NOG = 8
NJC = INNER // P
LN_EPS = 1e-5
EPS = 1e-6

ENGS = ("sync", "gpsimd", "vector", "scalar", "tensor")


def _finish(nc, block, ops, barrier, cleanup=None):
    if cleanup is not None:
        cleanup()
    @block.sync
    def _(e):
        for f in ops["sync"]:
            f(e)

    @block.gpsimd
    def _(e):
        for f in ops["gpsimd"]:
            f(e)

    @block.vector
    def _(e):
        for f in ops["vector"]:
            f(e)

    @block.scalar
    def _(e):
        for f in ops["scalar"]:
            f(e)

    @block.tensor
    def _(e):
        for f in ops["tensor"]:
            f(e)
    return nc


class Ctr:
    def __init__(self, nc, stack, name):
        self.h = stack.enter_context(nc.semaphore(name))
        self.n = 0

    def bump(self, k=1):
        self.n += k
        return self.n


class DSem:

    def __init__(self, nc, stack, name, k=1):
        self.sems = [stack.enter_context(nc.semaphore(f"{name}{i}")) for i in range(k)]
        self.n = [0] * k
        self.k = k

    def issue(self, em, eng, fn, slot=0):
        s = slot % self.k
        em(eng, lambda e, f=fn, h=self.sems[s]: f(e).then_inc(h, 16))
        self.n[s] += 16
        return (s, self.n[s])

    def wait(self, em, eng, token):
        if token is None:
            return
        s, v = token
        if v > 0:
            em(eng, lambda e, h=self.sems[s], vv=v: e.wait_ge(h, vv))

    def wait_all(self, em, eng):
        for s in range(self.k):
            if self.n[s] > 0:
                em(eng, lambda e, h=self.sems[s], vv=self.n[s]: e.wait_ge(h, vv))


def build(debug=False, stop_after=None):
    nc = bass.Bass("TRN2", target_bir_lowering=False, debug=False,
                   detect_race_conditions=False)

    dt_x = nc.dram_tensor("x", [T, D], F32, kind="ExternalInput").ap()
    dt_qkvwT = nc.dram_tensor("qkvwT", [D, 3 * D], F32, kind="ExternalInput").ap()
    dt_qkvb = nc.dram_tensor("qkv_b", [1, 3 * D], F32, kind="ExternalInput").ap()
    dt_qkbc = nc.dram_tensor("qkb_cols", [P, 24], F32, kind="ExternalInput").ap()
    dt_projwT = nc.dram_tensor("projwT", [D, D], F32, kind="ExternalInput").ap()
    dt_projb = nc.dram_tensor("proj_b", [1, D], F32, kind="ExternalInput").ap()
    dt_pmT = nc.dram_tensor("pmT", [HD, HD], F32, kind="ExternalInput").ap()
    dt_wgm = nc.dram_tensor("wg_mine", [P, INNER], F32, kind="ExternalInput").ap()
    dt_wum = nc.dram_tensor("wu_mine", [P, INNER], F32, kind="ExternalInput").ap()
    dt_wdm = nc.dram_tensor("wd_mine", [4 * P, D], F32, kind="ExternalInput").ap()
    dt_out = nc.dram_tensor("out", [T, D], F32, kind="ExternalOutput").ap()

    ik = "ExternalOutput" if debug else "Internal"
    d_xn = nc.dram_tensor("xn_d", [T, D], BF16, kind=ik).ap()
    d_xq = nc.dram_tensor("xq_d", [T, D], BF16, kind=ik).ap()
    d_qp = nc.dram_tensor("qp_d", [T, D], BF16, kind=ik).ap()
    d_a = nc.dram_tensor("a_d", [T, D], BF16, kind=ik).ap()
    d_h = nc.dram_tensor("h_d", [T, INNER], F32, kind="Internal").ap()
    d_hq = nc.dram_tensor("hq_d", [T, INNER], BF16, kind=ik).ap()
    d_wqg = nc.dram_tensor("wqg_full", [D, INNER], BF16, kind="Internal",
                           addr_space="Shared").ap()
    d_wqu = nc.dram_tensor("wqu_d", [D, INNER], BF16, kind="Internal",
                           addr_space="Shared").ap()
    d_wqd = nc.dram_tensor("wqd_d", [INNER, D], BF16, kind="Internal",
                           addr_space="Shared").ap()
    d_wqg_m = nc.dram_tensor("wqg_m", [P, INNER], BF16, kind="Internal").ap()
    d_wqu_m = nc.dram_tensor("wqu_m", [P, INNER], BF16, kind="Internal").ap()
    d_wqd_m = nc.dram_tensor("wqd_m", [4 * P, D], BF16, kind="Internal").ap()
    d_wqgdump = (nc.dram_tensor("wqg_d", [D, INNER], BF16, kind=ik).ap()
                 if debug else None)
    CCW = 1035
    d_ccin = nc.dram_tensor("cc_in", [P, CCW], F32, kind="Internal").ap()
    d_ccout = nc.dram_tensor("cc_out", [P, CCW], F32, kind="Internal").ap()
    d_ccdump = nc.dram_tensor("cc_dump", [P, CCW], F32, kind=ik).ap()

    ops = {e: [] for e in ENGS}

    def em(eng, fn):
        ops[eng].append(fn)

    def emi(eng, fn, ctr, k=1):
        em(eng, lambda e, f=fn, h=ctr.h, kk=k: f(e).then_inc(h, kk))
        return ctr.bump(k)

    def vsp():
        em("vector", lambda e: e.tensor_copy(spacer_t[:, 0:1], spacer_t[:, 1:2]))

    def wait(eng, ctr, val):
        if val > 0:
            em(eng, lambda e, h=ctr.h, v=val: e.wait_ge(h, v))

    with ExitStack() as stk:
        bar = Ctr(nc, stk, "bar")
        cv = Ctr(nc, stk, "cv")
        ca = Ctr(nc, stk, "ca")
        ct = Ctr(nc, stk, "ct")
        cg = Ctr(nc, stk, "cg")
        ccs = Ctr(nc, stk, "ccs")
        dss = []

        def mkds(name, k=1, eng="sync"):
            d = DSem(nc, stk, name, k)
            dss.append((d, eng))
            return d

        ds_p0 = mkds("dp0")
        ds_x = mkds("dx", 2)
        ds_tr = mkds("dtr", 2)
        ds_wt = mkds("dwt", 2)
        ds_fm1 = mkds("dfm1")
        ds_st = mkds("dst", 2)
        ds_h2 = mkds("dh2", 2)
        ds_fm2 = mkds("dfm2")
        ds_wd = mkds("dwd")
        ds_b2 = mkds("db2", 2)
        ds_x2 = mkds("dx2", 2)
        ds_qkvw = mkds("dqkvw", 1, "gpsimd")
        ds_xn = mkds("dxn", 2, "gpsimd")
        ds_xq = mkds("dxq", 2, "gpsimd")
        ds_qp = mkds("dqp", 2, "gpsimd")
        ds_cc = mkds("dcc", 1, "gpsimd")
        ds_wq = mkds("dwq", 2, "sync")
        ds_h = mkds("dh", 2, "gpsimd")
        ds_hq = mkds("dhq", 2, "gpsimd")
        ds_a = mkds("da", 2, "gpsimd")
        ds_pj = mkds("dpj", 1, "gpsimd")
        ds_out = mkds("dout", 2, "gpsimd")
        _sem_ids = [c.h.num for c in (bar, cv, ca, ct, cg, ccs)]
        for d, _ in dss:
            _sem_ids += [s.num for s in d.sems]
        _sem_range = range(min(_sem_ids), max(_sem_ids) + 1)
        em("gpsimd", lambda e, r=_sem_range: e.sem_clear(r))
        for _e in ENGS:
            if _e != "gpsimd":
                for _ in range(1024):
                    em(_e, lambda en: en.nop())
            em(_e, lambda en, h=bar.h: en.sem_inc(h, 1))
        bar.n += len(ENGS)
        _tgt0 = bar.n
        for _e in ENGS:
            wait(_e, bar, _tgt0)

        def barrier():
            for d, deng in dss:
                d.wait_all(em, deng)
            for e in ENGS:
                em(e, lambda en, h=bar.h: en.sem_inc(h, 1))
            bar.n += len(ENGS)
            tgt = bar.n
            for e in ENGS:
                wait(e, bar, tgt)

        def _cleanup():
            barrier()
            em("gpsimd", lambda e, r=_sem_range: e.dma_reset(r))
            em("gpsimd", lambda e, r=_sem_range: e.sem_clear(r))

        consts = stk.enter_context(nc.sbuf_tensor("consts", [P, 8], F32))
        CEPSLN = 1
        ones1 = stk.enter_context(nc.sbuf_tensor("ones1", [1, P], F32))
        ones128 = stk.enter_context(nc.sbuf_tensor("ones128", [P, 1], F32))
        hb = stk.enter_context(nc.sbuf_tensor("hb", [P, 2], BF16))
        bdpm = stk.enter_context(nc.sbuf_tensor("bdpm", [P, P], BF16))
        qkbs = stk.enter_context(nc.sbuf_tensor("qkbs", [P, 16], F32))
        vbias = stk.enter_context(nc.sbuf_tensor("vbias", [P, D], F32))
        pbias = stk.enter_context(nc.sbuf_tensor("pbias", [P, D], F32))
        rs_a = stk.enter_context(nc.sbuf_tensor("rs_a", [P, NT], F32))
        hmax = stk.enter_context(nc.sbuf_tensor("hmax", [P, NT], F32))
        sh_all = stk.enter_context(nc.sbuf_tensor("sh_all", [P, NT], F32))
        cgate = stk.enter_context(nc.sbuf_tensor("cgate", [P, NT], F32))
        cup = stk.enter_context(nc.sbuf_tensor("cup", [P, NT], F32))
        cdown = stk.enter_context(nc.sbuf_tensor("cdown", [P, NT], F32))
        scr16 = stk.enter_context(nc.sbuf_tensor("scr16", [P, NT], F32))
        ws_bc = stk.enter_context(nc.sbuf_tensor("ws_bc", [P, 3], F32))
        rw_bc = stk.enter_context(nc.sbuf_tensor("rw_bc", [P, 3], F32))
        ws11 = stk.enter_context(nc.sbuf_tensor("ws11", [1, 4], F32))
        kvr = stk.enter_context(nc.sbuf_tensor("kvr", [P, 8 * 130], BF16))
        spacer_t = stk.enter_context(nc.sbuf_tensor("spacer_t", [P, 2], F32))
        cc_sb = stk.enter_context(nc.sbuf_tensor("cc_sb", [P, CCW], F32))

        block = stk.enter_context(nc.Block())

        g_ccback = 0

        with ExitStack() as p0s:
            en = p0s.enter_context
            qb_row = en(nc.sbuf_tensor("qb_row", [1, 3 * D], F32))
            pb_row = en(nc.sbuf_tensor("pb_row", [1, D], F32))
            qkbc = en(nc.sbuf_tensor("qkbc", [P, 24], F32))
            pm2 = en(nc.sbuf_tensor("pm2", [P, HD], F32))
            pm2b = en(nc.sbuf_tensor("pm2b", [P, HD], BF16))
            ps_b1 = en(nc.psum_tensor("ps_b1", [P, 512], F32))
            ps_b2 = en(nc.psum_tensor("ps_b2", [P, 512], F32))

            ds_p0.issue(em, "sync", lambda e: e.dma_start(qb_row[:, :], dt_qkvb[:, :]))
            ds_p0.issue(em, "sync", lambda e: e.dma_start(pb_row[:, :], dt_projb[:, :]))
            ds_p0.issue(em, "sync", lambda e: e.dma_start(qkbc[:, :], dt_qkbc[:, :]))
            ds_p0.issue(em, "sync", lambda e: e.dma_start(pm2[0:64, :], dt_pmT[:, :]))
            ds_p0.issue(em, "sync", lambda e: e.dma_start(pm2[64:128, :], dt_pmT[:, :]))

            em("vector", lambda e: e.memset(spacer_t[:, :], 0.0))
            em("vector", lambda e: e.memset(consts[:, :], 0.0))
            em("vector", lambda e: e.memset(consts[:, CEPSLN:CEPSLN + 1], LN_EPS))
            em("vector", lambda e: e.memset(ones1[:, :], 1.0))
            em("vector", lambda e: e.memset(ones128[:, :], 1.0))
            em("vector", lambda e: e.memset(hb[:, :], 0.0))
            em("vector", lambda e: e.memset(hb[0:64, 0:1], 0.5))
            em("vector", lambda e: e.memset(hb[64:128, 1:2], 0.5))
            em("vector", lambda e: e.memset(hmax[:, :], 0.0))
            em("vector", lambda e: e.memset(bdpm[:, :], 0.0))
            ds_p0.wait_all(em, "vector")
            em("vector", lambda e: e.tensor_scalar_mul(qkbs[:, :], qkbc[:, 0:16], 0.125))
            em("vector", lambda e: e.tensor_copy(pm2b[:, :], pm2[:, :]))
            vsp()
            em("vector", lambda e: e.tensor_copy(bdpm[0:64, 0:64], pm2b[0:64, :]))
            v_setup = emi("vector", lambda e: e.tensor_copy(bdpm[64:128, 64:128], pm2b[64:128, :]), cv)

            ds_p0.wait_all(em, "tensor")
            wait("tensor", cv, v_setup)
            for half in range(2):
                em("tensor", lambda e, h=half: e.matmul(
                    ps_b1[:, :], ones1[:, :],
                    qb_row[0:1, 2 * D + 512 * h:2 * D + 512 * (h + 1)], start=True, stop=True))
                tb = emi("tensor", lambda e, h=half: e.matmul(
                    ps_b2[:, :], ones1[:, :], pb_row[0:1, 512 * h:512 * (h + 1)],
                    start=True, stop=True), ct)
                wait("vector", ct, tb)
                em("vector", lambda e, h=half: e.tensor_copy(vbias[:, 512 * h:512 * (h + 1)], ps_b1[:, :]))
                vb = emi("vector", lambda e, h=half: e.tensor_copy(
                    pbias[:, 512 * h:512 * (h + 1)], ps_b2[:, :]), cv)
                wait("tensor", cv, vb)

        if stop_after == "P0":
            return _finish(nc, block, ops, barrier, _cleanup)
        with ExitStack() as a0s:
            en = a0s.enter_context
            qkvwT_sb = en(nc.sbuf_tensor("qkvwT_sb", [P, DC, 3 * D], BF16))
            xn_fm = en(nc.sbuf_tensor("xn_fm", [P, DC, T], BF16))
            x_t = en(nc.sbuf_tensor("x_t", [P, 2, D], F32))
            xc_t = en(nc.sbuf_tensor("xc_t", [P, D], F32))
            sq_t = en(nc.sbuf_tensor("sq_t", [P, D], F32))
            xnf_t = en(nc.sbuf_tensor("xnf_t", [P, D], F32))
            u_t = en(nc.sbuf_tensor("u_t", [P, D], F32))
            qi_t = en(nc.sbuf_tensor("qi_t", [P, D], I32))
            c2_t = en(nc.sbuf_tensor("c2_t", [P, D], F32))
            ii_t = en(nc.sbuf_tensor("ii_t", [P, D], F32))
            xnb_t = en(nc.sbuf_tensor("xnb_t", [P, 2, D], BF16))
            xqb_t = en(nc.sbuf_tensor("xqb_t", [P, 2, D], BF16))
            stats = en(nc.sbuf_tensor("stats", [P, 16], F32))
            v_ext = en(nc.sbuf_tensor("v_ext", [P, NB, 8 * 129], BF16))
            vscr = en(nc.sbuf_tensor("vscr", [P, 2, 512], F32))
            dn_t = en(nc.sbuf_tensor("dn_t", [P, 2, 512], F32))
            dnsqb = en(nc.sbuf_tensor("dnsqb", [P, 2, 512], BF16))
            dnb = en(nc.sbuf_tensor("dnb", [P, 2, 512], BF16))
            comb2 = en(nc.sbuf_tensor("comb2", [P, 2, 512], F32))
            comb3 = en(nc.sbuf_tensor("comb3", [P, 2, 512], F32))
            ex_t = en(nc.sbuf_tensor("ex_t", [P, 2, 512], F32))
            kpq = en(nc.sbuf_tensor("kpq", [P, 2, 512], BF16))
            ns_sb = en(nc.sbuf_tensor("ns_sb", [P, 2, 8], F32))
            rm8 = en(nc.sbuf_tensor("rm8", [P, 2, 8], F32))
            rm2 = en(nc.sbuf_tensor("rm2", [P, 2, 8], F32))
            qp_blk = en(nc.sbuf_tensor("qp_blk", [P, 2, NB, D], BF16))

            pp = [en(nc.psum_tensor(f"pp{i}", [P, 512], F32)) for i in range(2)]
            pc = [en(nc.psum_tensor(f"pc{i}", [P, 512], F32)) for i in range(2)]
            pns = en(nc.psum_tensor("pns", [P, 512], F32))
            pkv = [en(nc.psum_tensor(f"pkv{i}", [P, 512], F32)) for i in range(3)]

            qkvwT_v = dt_qkvwT.rearrange("(a p) c -> p a c", p=P)
            wait("gpsimd", cv, vb)
            for dc in range(DC):
                ds_qkvw.issue(em, "gpsimd", lambda e, d=dc: e.dma_start(
                    qkvwT_sb[:, d, :], qkvwT_v[:, d, :]))

            xn_store = {}
            xq_store = {}
            ln_done = [0, 0]
            var_done = [0, 0]
            ppfree = [0, 0]
            ppfill = [0, 0]
            pcfree = [0, 0]
            vext_ok = [0] * NB
            qp_store = [None, None]
            ns_copied = 0
            last_kv = 0
            kv_blk_done = 0
            kpq_free = [0, 0]
            dn_free = [0, 0]
            pi = ci = di = 0

            for b in range(NB):
                for tl in range(NB):
                    tt = NB * b + tl
                    xi = tt % 2
                    sg0 = 10 + 2 * xi
                    wait("gpsimd", cv, ln_done[xi])
                    x_tok = ds_x.issue(em, "gpsimd", lambda e, t=tt, i=xi: e.dma_start(
                        x_t[:, i, :], dt_x[P * t:P * (t + 1), :]), tt)
                    ds_x.wait(em, "scalar", x_tok)
                    wait("scalar", cv, var_done[xi])
                    em("scalar", lambda e, i=xi, s=sg0: e.activation(
                        xc_t[:, :], x_t[:, i, :], AF.Copy, accum_out=stats[:, s:s + 1]))
                    a1 = emi("scalar", lambda e, i=xi, s=sg0: e.activation(
                        sq_t[:, :], x_t[:, i, :], AF.Square,
                        accum_out=stats[:, s + 1:s + 2]), ca)
                    wait("vector", ca, a1)
                    em("vector", lambda e, s=sg0: e.tensor_scalar_mul(
                        stats[:, 1:2], stats[:, s:s + 1], 1.0 / D))
                    vsp()
                    em("vector", lambda e: e.tensor_tensor(
                        stats[:, 2:3], stats[:, 1:2], stats[:, 1:2], ALU.mult))
                    vsp()
                    v2 = emi("vector", lambda e, s=sg0: e.tensor_scalar(
                        stats[:, 3:4], stats[:, s + 1:s + 2], 1.0 / D, stats[:, 2:3],
                        ALU.mult, ALU.subtract), cv)
                    var_done[xi] = v2
                    wait("scalar", cv, v2)
                    a2 = emi("scalar", lambda e: e.activation(
                        stats[:, 4:5], stats[:, 3:4], AF.Sqrt,
                        bias=consts[:, CEPSLN:CEPSLN + 1]), ca)
                    wait("vector", ca, a2)
                    em("vector", lambda e: e.reciprocal(stats[:, 5:6], stats[:, 4:5]))
                    vsp()
                    em("vector", lambda e, i=xi: e.tensor_scalar(
                        xnf_t[:, :], x_t[:, i, :], stats[:, 1:2], stats[:, 5:6],
                        ALU.subtract, ALU.mult))
                    ds_xn.wait(em, "vector", xn_store.get(tt - 2))
                    v3 = emi("vector", lambda e, i=xi: e.tensor_copy(xnb_t[:, i, :], xnf_t[:, :]), cv)
                    wait("gpsimd", cv, v3)
                    xn_store[tt] = ds_xn.issue(em, "gpsimd", lambda e, t=tt, i=xi: e.dma_start(
                        d_xn[P * t:P * (t + 1), :], xnb_t[:, i, :]), tt)
                    em("vector", lambda e: e.tensor_reduce(
                        stats[:, 6:7], xnf_t[:, :], AX.X, ALU.max,
                        apply_absolute_value=True))
                    vsp()
                    em("vector", lambda e: e.tensor_scalar_max(stats[:, 7:8], stats[:, 6:7], 1e-5))
                    vsp()
                    em("vector", lambda e: e.reciprocal(stats[:, 8:9], stats[:, 7:8]))
                    vsp()
                    em("vector", lambda e: e.tensor_scalar_mul(stats[:, 9:10], stats[:, 8:9], 127.0))
                    vsp()
                    em("vector", lambda e, t=tt: e.reciprocal(rs_a[:, t:t + 1], stats[:, 9:10]))
                    em("vector", lambda e: e.tensor_scalar(
                        u_t[:, :], xnf_t[:, :], stats[:, 9:10], 256.0, ALU.mult, ALU.add))
                    em("vector", lambda e: e.tensor_copy(qi_t[:, :], u_t[:, :]))
                    em("vector", lambda e: e.tensor_scalar_add(c2_t[:, :], qi_t[:, :], -256.0))
                    em("vector", lambda e: e.scalar_tensor_tensor(
                        ii_t[:, :], u_t[:, :], 256.5, c2_t[:, :], ALU.subtract, ALU.is_ge))
                    ds_xq.wait(em, "vector", xq_store.get(tt - 2))
                    ln_done[xi] = emi("vector", lambda e, i=xi: e.tensor_tensor(
                        xqb_t[:, i, :], c2_t[:, :], ii_t[:, :], ALU.add), cv)
                    wait("gpsimd", cv, ln_done[xi])
                    xq_store[tt] = ds_xq.issue(em, "gpsimd", lambda e, t=tt, i=xi: e.dma_start(
                        d_xq[P * t:P * (t + 1), :], xqb_t[:, i, :]), tt)

                if stop_after == f"LN{b}":
                    return _finish(nc, block, ops, barrier, _cleanup)
                ds_xn.wait_all(em, "sync")
                for dc in range(DC):
                    tr_tok = ds_tr.issue(em, "sync", lambda e, d=dc, bb=b: e.dma_start(
                        xn_fm[:, d, 512 * bb:512 * (bb + 1)],
                        d_xn[512 * bb:512 * (bb + 1), P * d:P * (d + 1)],
                        transpose=True), b)

                if stop_after == f"TR{b}":
                    return _finish(nc, block, ops, barrier, _cleanup)
                ds_tr.wait(em, "tensor", tr_tok)
                if b == 0:
                    ds_qkvw.wait_all(em, "tensor")
                wait("vector", ct, kv_blk_done)
                if b == 0:
                    em("vector", lambda e: e.memset(
                        v_ext[:, :, :].rearrange("p b (a c) -> p b a c", c=129)[:, :, :, 128:129], 1.0))
                if stop_after == f"VS{b}":
                    return _finish(nc, block, ops, barrier, _cleanup)
                for tl in range(NB):
                    for vo in range(2):
                        i = pi; pi ^= 1
                        wait("tensor", cv, ppfree[i])
                        for dc in range(DC - 1):
                            em("tensor", lambda e, d=dc, bb=b, t=tl, v=vo, i_=i: e.matmul(
                                pp[i_][:, :],
                                xn_fm[:, d, 512 * bb + P * t:512 * bb + P * (t + 1)],
                                qkvwT_sb[:, d, 2 * D + 512 * v:2 * D + 512 * (v + 1)],
                                start=(d == 0), stop=False))
                        ppfill[i] = emi("tensor", lambda e, d=DC - 1, bb=b, t=tl, v=vo, i_=i: e.matmul(
                            pp[i_][:, :],
                            xn_fm[:, d, 512 * bb + P * t:512 * bb + P * (t + 1)],
                            qkvwT_sb[:, d, 2 * D + 512 * v:2 * D + 512 * (v + 1)],
                            start=False, stop=True), ct)
                        if stop_after == f"VM{b}{tl}{vo}":
                            return _finish(nc, block, ops, barrier, _cleanup)
                        wait("vector", ct, ppfill[i])
                        if stop_after == f"VW{b}{tl}{vo}":
                            return _finish(nc, block, ops, barrier, _cleanup)
                        em("vector", lambda e, v=vo, i_=i: e.tensor_tensor(
                            vscr[:, i_, :], pp[i_][:, :], vbias[:, 512 * v:512 * (v + 1)], ALU.add))
                        if stop_after == f"VT{b}{tl}{vo}":
                            return _finish(nc, block, ops, barrier, _cleanup)
                        ppfree[i] = emi("vector", lambda e, t=tl, v=vo, i_=i: e.tensor_copy(
                            v_ext[:, t, :].rearrange("p (a c) -> p a c", c=129)[:, 4 * v:4 * (v + 1), 0:128],
                            vscr[:, i_, :].rearrange("p (a c) -> p a c", c=128)), cv)
                        if stop_after == f"VC{b}{tl}{vo}":
                            return _finish(nc, block, ops, barrier, _cleanup)
                vext_ok[b] = ppfree[i]

                if stop_after == f"V{b}":
                    return _finish(nc, block, ops, barrier, _cleanup)
                for oc in range(16):
                    ocg = 16 * b + oc
                    i = pi; pi ^= 1
                    wait("tensor", cv, ppfree[i])
                    for dc in range(DC - 1):
                        em("tensor", lambda e, d=dc, o=oc, bb=b, i_=i: e.matmul(
                            pp[i_][:, :],
                            qkvwT_sb[:, d, P * o:P * (o + 1)],
                            xn_fm[:, d, 512 * bb:512 * (bb + 1)],
                            start=(d == 0), stop=False))
                    ppfill[i] = emi("tensor", lambda e, d=DC - 1, o=oc, bb=b, i_=i: e.matmul(
                        pp[i_][:, :],
                        qkvwT_sb[:, d, P * o:P * (o + 1)],
                        xn_fm[:, d, 512 * bb:512 * (bb + 1)],
                        start=False, stop=True), ct)

                    dd = di; di ^= 1
                    wait("vector", ct, max(ppfill[i], dn_free[dd]))
                    em("vector", lambda e, o=oc, i_=i, d=dd: e.tensor_scalar(
                        dn_t[:, d, :], pp[i_][:, :], 0.125, qkbs[:, o:o + 1], ALU.mult, ALU.add))
                    ppfree[i] = emi("vector", lambda e, d=dd: e.tensor_copy(
                        dnb[:, d, :], dn_t[:, d, :]), cv)
                    wait("scalar", cv, ppfree[i])
                    a_sq = emi("scalar", lambda e, d=dd: e.activation(
                        dnsqb[:, d, :], dn_t[:, d, :], AF.Square), ca)

                    cci = ci; ci ^= 1
                    nscol = (ocg % 64) * 8
                    wait("tensor", ca, a_sq)
                    wait("tensor", cv, max(pcfree[cci], ns_copied))
                    for tl in range(NB):
                        em("tensor", lambda e, t=tl, d=dd, c=cci: e.matmul(
                            pc[c][:, P * t:P * (t + 1)], dnb[:, d, P * t:P * (t + 1)],
                            bdpm[:, :], start=True, stop=True))
                    for tl in range(NB - 1):
                        em("tensor", lambda e, t=tl, d=dd, nco=nscol: e.matmul(
                            pns[:, nco + 2 * t:nco + 2 * (t + 1)],
                            dnsqb[:, d, P * t:P * (t + 1)], hb[:, :], start=True, stop=True))
                    pcfill = emi("tensor", lambda e, t=NB - 1, d=dd, nco=nscol: e.matmul(
                        pns[:, nco + 2 * t:nco + 2 * (t + 1)],
                        dnsqb[:, d, P * t:P * (t + 1)], hb[:, :], start=True, stop=True), ct)
                    dn_free[dd] = pcfill

                    wait("vector", ct, pcfill)
                    ns_copied = emi("vector", lambda e, d=dd, nco=nscol: e.tensor_copy(
                        ns_sb[:, d, :], pns[:, nco:nco + 8]), cv)
                    em("vector", lambda e, d=dd, c=cci: e.reduce_max(
                        rm8[:, d, :],
                        pc[c][:, :].rearrange("p (a c) -> p a c", c=64), AX.X))
                    for tl in range(NB):
                        for h2 in range(2):
                            if tl == NB - 1 and h2 == 1:
                                pcfree[cci] = emi("vector", lambda e, t=tl, h=h2, d=dd, c=cci: e.tensor_single_scalar(
                                    comb2[:, d, P * t + 64 * h:P * t + 64 * (h + 1)],
                                    pc[c][:, P * t + 64 * h:P * t + 64 * (h + 1)],
                                    ns_sb[:, d, 2 * t + h:2 * t + h + 1], ALU.subtract), cv)
                            else:
                                em("vector", lambda e, t=tl, h=h2, d=dd, c=cci: e.tensor_single_scalar(
                                    comb2[:, d, P * t + 64 * h:P * t + 64 * (h + 1)],
                                    pc[c][:, P * t + 64 * h:P * t + 64 * (h + 1)],
                                    ns_sb[:, d, 2 * t + h:2 * t + h + 1], ALU.subtract))
                    em("vector", lambda e, d=dd: e.tensor_tensor(
                        rm2[:, d, :], rm8[:, d, :], ns_sb[:, d, :], ALU.subtract))
                    vsp()
                    for tl in range(NB):
                        for h2 in range(2):
                            if tl == NB - 1 and h2 == 1:
                                v_c3 = emi("vector", lambda e, t=tl, h=h2, d=dd: e.tensor_single_scalar(
                                    comb3[:, d, P * t + 64 * h:P * t + 64 * (h + 1)],
                                    comb2[:, d, P * t + 64 * h:P * t + 64 * (h + 1)],
                                    rm2[:, d, 2 * t + h:2 * t + h + 1], ALU.subtract), cv)
                            else:
                                em("vector", lambda e, t=tl, h=h2, d=dd: e.tensor_single_scalar(
                                    comb3[:, d, P * t + 64 * h:P * t + 64 * (h + 1)],
                                    comb2[:, d, P * t + 64 * h:P * t + 64 * (h + 1)],
                                    rm2[:, d, 2 * t + h:2 * t + h + 1], ALU.subtract))
                    wait("scalar", cv, v_c3)
                    a_ex = emi("scalar", lambda e, d=dd: e.activation(
                        ex_t[:, d, :], comb3[:, d, :], AF.Exp), ca)
                    wait("vector", ca, a_ex)
                    if oc < 8:
                        if b >= 2 and oc == 0:
                            ds_qp.wait(em, "vector", qp_store[b % 2])
                        v_qp = emi("vector", lambda e, d=dd, bb=b, o=oc: e.tensor_scalar(
                            qp_blk[:, bb % 2, :, P * o:P * (o + 1)],
                            ex_t[:, d, :].rearrange("p (a c) -> p a c", c=P),
                            0.125, EPS, ALU.mult, ALU.add), cv)
                        if oc == 7:
                            wait("gpsimd", cv, v_qp)
                            qp_store[b % 2] = ds_qp.issue(em, "gpsimd", lambda e, bb=b: e.dma_start(
                                d_qp[512 * bb:512 * (bb + 1), :].rearrange("(a p) c -> p a c", p=P),
                                qp_blk[:, bb % 2, :, :]), b)
                        if stop_after == f"QP{b}{oc}":
                            return _finish(nc, block, ops, barrier, _cleanup)
                    else:
                        pr = oc - 8
                        kk = dd
                        wait("vector", ct, kpq_free[kk])
                        v_kp = emi("vector", lambda e, d=dd, k=kk: e.tensor_scalar(
                            kpq[:, k, :], ex_t[:, d, :], 0.125, EPS, ALU.mult, ALU.add), cv)
                        wait("tensor", cv, max(v_kp, vext_ok[b]))
                        def kvmm(e, t, k, p_, bb):
                            return e.matmul(
                                pkv[p_ // 3][:, 129 * (p_ % 3):129 * (p_ % 3) + 129],
                                kpq[:, k, P * t:P * (t + 1)],
                                v_ext[:, t, 129 * p_:129 * (p_ + 1)],
                                start=(bb == 0 and t == 0 and p_ % 3 == 0),
                                stop=(bb == NB - 1 and t == NB - 1 and (p_ % 3 == 2 or p_ == 7)))
                        for tl in range(NB - 1):
                            em("tensor", lambda e, t=tl, k=kk, p_=pr, bb=b, f=kvmm: f(e, t, k, p_, bb))
                        kpq_free[kk] = emi("tensor", lambda e, t=NB - 1, k=kk, p_=pr, bb=b, f=kvmm: f(e, t, k, p_, bb), ct)
                        if oc == 15:
                            kv_blk_done = kpq_free[kk]
                        if b == NB - 1 and oc == 15:
                            last_kv = kpq_free[kk]

            if stop_after == "A0":
                return _finish(nc, block, ops, barrier, _cleanup)
            w1_t = en(nc.sbuf_tensor("w1_t", [P, 2, 512], F32))
            w1part = en(nc.sbuf_tensor("w1part", [P, 8], F32))
            w1_done = [0, 0]
            w1srcs = (
                [dt_wgm[:, 512 * j:512 * (j + 1)] for j in range(8)],
                [dt_wum[:, 512 * j:512 * (j + 1)] for j in range(8)],
                [dt_wdm[P * (j // 2):P * (j // 2 + 1), 512 * (j % 2):512 * (j % 2 + 1)]
                 for j in range(8)],
            )
            for mi_ in range(3):
                for j in range(8):
                    i = j % 2
                    wait("sync", cv, w1_done[i])
                    w1_tok = ds_wt.issue(em, "sync", lambda e, s=w1srcs[mi_][j], i_=i: e.dma_start(
                        w1_t[:, i_, :], s), j)
                    ds_wt.wait(em, "vector", w1_tok)
                    w1_done[i] = emi("vector", lambda e, j_=j, i_=i: e.tensor_reduce(
                        w1part[:, j_:j_ + 1], w1_t[:, i_, :], AX.X, ALU.add,
                        apply_absolute_value=True), cv)
                vsp()
                em("vector", lambda e, m=mi_: e.reduce_sum(
                    cc_sb[:, 1032 + m:1033 + m], w1part[:, 0:8], AX.X))
            wait("vector", ct, last_kv)
            em("vector", lambda e: e.tensor_copy(cc_sb[:, 0:387], pkv[0][:, 0:387]))
            em("vector", lambda e: e.tensor_copy(cc_sb[:, 387:774], pkv[1][:, 0:387]))
            v_cc = emi("vector", lambda e: e.tensor_copy(cc_sb[:, 774:1032], pkv[2][:, 0:258]), cv)
            wait("gpsimd", cv, v_cc)
            cc_tok = ds_cc.issue(em, "gpsimd", lambda e: e.dma_start(d_ccin[:, :], cc_sb[:, :]))
            ds_cc.wait(em, "gpsimd", cc_tok)
            emi("gpsimd", lambda e: e.collective_compute(
                "AllReduce", ALU.add, replica_groups=[list(range(NCORES))],
                ins=[d_ccin.opt()], outs=[d_ccout.opt()]), ccs, 1)
            wait("gpsimd", ccs, ccs.n)
            g_ccback = ds_cc.issue(em, "gpsimd", lambda e: e.dma_start(cc_sb[:, :], d_ccout[:, :]))
            wait("sync", ct, last_kv)
            ds_qp.wait_all(em, "sync")
            ds_xn.wait_all(em, "sync")
            ds_xq.wait_all(em, "sync")

        if stop_after == "AR":
            return _finish(nc, block, ops, barrier, _cleanup)
        with ExitStack() as ws:
            en = ws.enter_context
            w_t = en(nc.sbuf_tensor("w_t", [P, 2, INNER], F32))
            v_s = en(nc.sbuf_tensor("v_s", [P, 2, INNER], F32))
            i1_s = en(nc.sbuf_tensor("i1_s", [P, 2, INNER], F32))
            i2_s = en(nc.sbuf_tensor("i2_s", [P, 2, INNER], F32))
            tern = en(nc.sbuf_tensor("tern", [P, 2, INNER], BF16))
            ps_s = en(nc.psum_tensor("ps_s", [P, 8], F32))
            ps_bb = en(nc.psum_tensor("ps_bb", [P, 8], F32))

            mats = [(dt_wgm, d_wqg_m, 1, INNER, 0), (dt_wum, d_wqu_m, 1, INNER, 1),
                    (dt_wdm, d_wqd_m, 4, D, 2)]
            red_done = [0, 0]
            tern_done = [0, 0]
            tn_used = [0, 0]
            tern_stored = [None, None]
            i1_done = [0, 0]
            i2_done = [0, 0]
            nel = float(D * INNER)
            ds_cc.wait(em, "tensor", g_ccback)
            for (src, dst, ntile, C, mi) in mats:
                t_ws = emi("tensor", lambda e, m=mi: e.matmul(
                    ps_s[0:1, m:m + 1], cc_sb[:, 1032 + m:1033 + m], ones128[:, :],
                    start=True, stop=True), ct)
                wait("vector", ct, t_ws)
                em("vector", lambda e, m=mi, nn=nel: e.tensor_scalar_mul(
                    ws11[0:1, 3:4], ps_s[0:1, m:m + 1], 1.0 / nn))
                vsp()
                em("vector", lambda e: e.tensor_scalar_max(ws11[0:1, 3:4], ws11[0:1, 3:4], 1e-5))
                vsp()
                v_s11 = emi("vector", lambda e, m=mi: e.reciprocal(
                    ws11[0:1, m:m + 1], ws11[0:1, 3:4]), cv)
                wait("tensor", cv, v_s11)
                t_bb = emi("tensor", lambda e, m=mi: e.matmul(
                    ps_bb[:, m:m + 1], ones1[:, :], ws11[0:1, m:m + 1],
                    start=True, stop=True), ct)
                wait("vector", ct, t_bb)
                em("vector", lambda e, m=mi: e.tensor_copy(ws_bc[:, m:m + 1], ps_bb[:, m:m + 1]))
                vsp()
                v_wsb = emi("vector", lambda e, m=mi: e.reciprocal(
                    rw_bc[:, m:m + 1], ws_bc[:, m:m + 1]), cv)

                for it in range(ntile):
                    i = it % 2
                    wait("sync", ca, tern_done[i])
                    wait("sync", cv, red_done[i])
                    wt_tok = ds_wt.issue(em, "sync", lambda e, s=src, t=it, i_=i, c=C: e.dma_start(
                        w_t[:, i_, 0:c], s[P * t:P * (t + 1), :]), it)
                    ds_wt.wait(em, "scalar", wt_tok)
                    wait("scalar", cv, max(v_wsb, i2_done[i]))
                    tern_done[i] = emi("scalar", lambda e, i_=i, c=C, m=mi: e.activation(
                        v_s[:, i_, 0:c], w_t[:, i_, 0:c], AF.Copy,
                        scale=ws_bc[:, m:m + 1]), ca)
                    wait("vector", ca, tern_done[i])
                    em("vector", lambda e, i_=i, c=C: e.tensor_single_scalar(
                        i1_s[:, i_, 0:c], v_s[:, i_, 0:c], 0.5, ALU.is_ge))
                    i2_done[i] = emi("vector", lambda e, i_=i, c=C: e.tensor_single_scalar(
                        i2_s[:, i_, 0:c], v_s[:, i_, 0:c], -0.5, ALU.is_le), cv)
                    ds_wq.wait(em, "vector", tern_stored[i])
                    v_tn = emi("vector", lambda e, i_=i, c=C: e.tensor_tensor(
                        tern[:, i_, 0:c], i1_s[:, i_, 0:c], i2_s[:, i_, 0:c], ALU.subtract), cv)
                    tn_used[i] = v_tn
                    wait("sync", cv, v_tn)
                    tern_stored[i] = ds_wq.issue(em, "sync", lambda e, d=dst, t=it, i_=i, c=C: e.dma_start(
                        d[P * t:P * (t + 1), :], tern[:, i_, 0:c]), it)
                ds_wq.wait_all(em, "gpsimd")
                mine, full = ((d_wqg_m, d_wqg), (d_wqu_m, d_wqu), (d_wqd_m, d_wqd))[mi]
                emi("gpsimd", lambda e, a=mine, o=full: e.collective_compute(
                    "AllGather", ALU.bypass, replica_groups=[list(range(NCORES))],
                    ins=[a.opt()], outs=[o.opt()]), ccs, 1)
            ccs_gathers = ccs.n
            v_wsb_last = v_wsb
            if debug:
                wait("sync", ccs, ccs_gathers)
                ds_wq.issue(em, "sync", lambda e: e.dma_start(
                    d_wqgdump[:, :], d_wqg[:, :]))
            if stop_after == "W":
                barrier()

        if stop_after == "W":
            return _finish(nc, block, ops, barrier, _cleanup)
        with ExitStack() as a1s:
            en = a1s.enter_context
            xq_fm = en(nc.sbuf_tensor("xq_fm", [P, DC, T], BF16))
            wgst = en(nc.sbuf_tensor("wgst", [P, 2, DC, 512], BF16))
            wust = en(nc.sbuf_tensor("wust", [P, 2, DC, 512], BF16))
            gate_f = en(nc.sbuf_tensor("gate_f", [P, 2, 512], F32))
            up_f = en(nc.sbuf_tensor("up_f", [P, 2, 512], F32))
            h_f = en(nc.sbuf_tensor("h_f", [P, 2, 512], F32))
            hred = en(nc.sbuf_tensor("hred", [P, 1], F32))
            psG = [en(nc.psum_tensor(f"psG{i}", [P, 512], F32)) for i in range(2)]
            psU = [en(nc.psum_tensor(f"psU{i}", [P, 512], F32)) for i in range(2)]

            ds_wq.wait_all(em, "sync")
            for dc in range(DC):
                ds_fm1.issue(em, "sync", lambda e, d=dc: e.dma_start(
                    xq_fm[:, d, :], d_xq[:, P * d:P * (d + 1)], transpose=True))
            wait("sync", ccs, ccs_gathers - 1)
            em("vector", lambda e: e.tensor_single_scalar(cgate[:, :], rs_a[:, :], rw_bc[:, 0:1], ALU.mult))
            v_cg = emi("vector", lambda e: e.tensor_single_scalar(
                cup[:, :], rs_a[:, :], rw_bc[:, 1:2], ALU.mult), cv)

            wq_g_v = d_wqg.rearrange("(a p) c -> p a c", p=P)
            wq_u_v = d_wqu.rearrange("(a p) c -> p a c", p=P)
            stage_free = [0, 0]
            psfree = [0, 0]
            hstore = [None, None]
            hfree = [0, 0]
            for og in range(NOG):
                i = og % 2
                wait("sync", ct, stage_free[i])
                ds_st.issue(em, "sync", lambda e, o=og, i_=i: e.dma_start(
                    wgst[:, i_, :, :], wq_g_v[:, :, 512 * o:512 * (o + 1)]), og)
                st_tok = ds_st.issue(em, "sync", lambda e, o=og, i_=i: e.dma_start(
                    wust[:, i_, :, :], wq_u_v[:, :, 512 * o:512 * (o + 1)]), og)
                for tt in range(NT):
                    j = tt % 2
                    if og == 0 and tt == 0:
                        ds_fm1.wait_all(em, "tensor")
                        wait("tensor", cv, v_wsb_last)
                    ds_st.wait(em, "tensor", st_tok)
                    wait("tensor", ca, psfree[j])
                    for dc in range(DC):
                        em("tensor", lambda e, d=dc, t=tt, i_=i, j_=j: e.matmul(
                            psG[j_][:, :], xq_fm[:, d, P * t:P * (t + 1)],
                            wgst[:, i_, d, :], start=(d == 0), stop=(d == DC - 1)))
                    for dc in range(DC - 1):
                        em("tensor", lambda e, d=dc, t=tt, i_=i, j_=j: e.matmul(
                            psU[j_][:, :], xq_fm[:, d, P * t:P * (t + 1)],
                            wust[:, i_, d, :], start=(d == 0), stop=False))
                    t_gu = emi("tensor", lambda e, d=DC - 1, t=tt, i_=i, j_=j: e.matmul(
                        psU[j_][:, :], xq_fm[:, d, P * t:P * (t + 1)],
                        wust[:, i_, d, :], start=False, stop=True), ct)
                    if tt == NT - 1:
                        stage_free[i] = t_gu
                    wait("scalar", ct, t_gu)
                    wait("scalar", cv, max(v_cg, hfree[j]))
                    em("scalar", lambda e, t=tt, j_=j: e.activation(
                        gate_f[:, j_, :], psG[j_][:, :], AF.Silu, scale=cgate[:, t:t + 1]))
                    psfree[j] = emi("scalar", lambda e, t=tt, j_=j: e.activation(
                        up_f[:, j_, :], psU[j_][:, :], AF.Copy, scale=cup[:, t:t + 1]), ca)
                    wait("vector", ca, psfree[j])
                    ds_h.wait(em, "vector", hstore[j])
                    em("vector", lambda e, j_=j: e.tensor_tensor(
                        h_f[:, j_, :], gate_f[:, j_, :], up_f[:, j_, :], ALU.mult))
                    em("vector", lambda e, j_=j: e.tensor_reduce(
                        hred[:, :], h_f[:, j_, :], AX.X, ALU.max, apply_absolute_value=True))
                    vsp()
                    hfree[j] = emi("vector", lambda e, t=tt: e.tensor_tensor(
                        hmax[:, t:t + 1], hmax[:, t:t + 1], hred[:, :], ALU.max), cv)
                    wait("gpsimd", cv, hfree[j])
                    hstore[j] = ds_h.issue(em, "gpsimd", lambda e, t=tt, o=og, j_=j: e.dma_start(
                        d_h[P * t:P * (t + 1), 512 * o:512 * (o + 1)], h_f[:, j_, :]), tt)
            vsp()
            em("vector", lambda e: e.tensor_scalar_max(scr16[:, :], hmax[:, :], 1e-5))
            vsp()
            em("vector", lambda e: e.reciprocal(scr16[:, :], scr16[:, :]))
            vsp()
            em("vector", lambda e: e.tensor_scalar_mul(sh_all[:, :], scr16[:, :], 127.0))
            vsp()
            em("vector", lambda e: e.reciprocal(scr16[:, :], sh_all[:, :]))
            vsp()
            emi("vector", lambda e: e.tensor_single_scalar(
                cdown[:, :], scr16[:, :], rw_bc[:, 2:3], ALU.mult), cv)
            barrier()

        if stop_after == "A1":
            return _finish(nc, block, ops, barrier, _cleanup)
        with ExitStack() as a2s:
            en = a2s.enter_context
            HW = 2048
            h_t = en(nc.sbuf_tensor("h_t", [P, 2, HW], F32))
            u2 = en(nc.sbuf_tensor("u2", [P, 2, HW], F32))
            qi2 = en(nc.sbuf_tensor("qi2", [P, 2, HW], I32))
            c22 = en(nc.sbuf_tensor("c22", [P, 2, HW], F32))
            ii2 = en(nc.sbuf_tensor("ii2", [P, 2, HW], F32))
            hqb = en(nc.sbuf_tensor("hqb", [P, 2, HW], BF16))
            qp_fm = en(nc.sbuf_tensor("qp_fm", [P, DC, T], BF16))
            a_blk = en(nc.sbuf_tensor("a_blk", [P, 2, D], BF16))
            nrm16 = en(nc.sbuf_tensor("nrm16", [P, 16], F32))
            nrm2 = en(nc.sbuf_tensor("nrm2", [P, 16], F32))
            rn16 = en(nc.sbuf_tensor("rn16", [P, 16], F32))
            psO = [en(nc.psum_tensor(f"psO{i}", [P, 512], F32)) for i in range(6)]

            done_i = [None, None]
            hq_done = [0, 0]

            def a2_iter(it):
                tt, hh = divmod(it, 2)
                i = it % 2
                ds_hq.wait(em, "sync", done_i[i])
                h2_tok = ds_h2.issue(em, "sync", lambda e, t=tt, h_=hh, i_=i: e.dma_start(
                    h_t[:, i_, :], d_h[P * t:P * (t + 1), HW * h_:HW * (h_ + 1)]), it)
                ds_h2.wait(em, "scalar", h2_tok)
                wait("scalar", cg, hq_done[i])
                a_u = emi("scalar", lambda e, t=tt, i_=i: e.activation(
                    u2[:, i_, :], h_t[:, i_, :], AF.Copy, bias=256.0,
                    scale=sh_all[:, t:t + 1]), ca)
                wait("vector", ca, a_u)
                wait("vector", cg, hq_done[i])
                em("vector", lambda e, i_=i: e.tensor_copy(qi2[:, i_, :], u2[:, i_, :]))
                em("vector", lambda e, i_=i: e.tensor_scalar_add(c22[:, i_, :], qi2[:, i_, :], -256.0))
                v_ii = emi("vector", lambda e, i_=i: e.scalar_tensor_tensor(
                    ii2[:, i_, :], u2[:, i_, :], 256.5, c22[:, i_, :],
                    ALU.subtract, ALU.is_ge), cv)
                wait("gpsimd", cv, v_ii)
                hq_done[i] = emi("gpsimd", lambda e, i_=i: e.tensor_tensor(
                    hqb[:, i_, :], c22[:, i_, :], ii2[:, i_, :], ALU.add), cg)
                done_i[i] = ds_hq.issue(em, "gpsimd", lambda e, t=tt, h_=hh, i_=i: e.dma_start(
                    d_hq[P * t:P * (t + 1), HW * h_:HW * (h_ + 1)], hqb[:, i_, :]), it)

            ds_cc.wait(em, "vector", g_ccback)
            if debug:
                ds_cc.wait(em, "gpsimd", g_ccback)
                ds_cc.issue(em, "gpsimd", lambda e: e.dma_start(d_ccdump[:, :], cc_sb[:, :]))
            em("vector", lambda e: e.memset(kvr[:, :], 0.0))
            for p_ in range(8):
                base = 129 * p_
                ko = 130 * p_
                em("vector", lambda e, b_=base, k_=ko: e.tensor_copy(
                    kvr[0:64, k_:k_ + 64], cc_sb[0:64, b_:b_ + 64]))
                em("vector", lambda e, b_=base, k_=ko: e.tensor_copy(
                    kvr[0:64, k_ + 64:k_ + 65], cc_sb[0:64, b_ + 128:b_ + 129]))
                em("vector", lambda e, b_=base, k_=ko: e.tensor_copy(
                    kvr[64:128, k_ + 65:k_ + 129], cc_sb[64:128, b_ + 64:b_ + 128]))
                if p_ == 7:
                    v_kvr = emi("vector", lambda e, b_=base, k_=ko: e.tensor_copy(
                        kvr[64:128, k_ + 129:k_ + 130], cc_sb[64:128, b_ + 128:b_ + 129]), cv)
                else:
                    em("vector", lambda e, b_=base, k_=ko: e.tensor_copy(
                        kvr[64:128, k_ + 129:k_ + 130], cc_sb[64:128, b_ + 128:b_ + 129]))

            for dc in range(DC):
                ds_fm2.issue(em, "sync", lambda e, d=dc: e.dma_start(
                    qp_fm[:, d, :], d_qp[:, P * d:P * (d + 1)], transpose=True))

            oset_free = [0, 0]
            astore = [None, None]

            def b1_iter(tt):
                s = tt % 2
                if tt == 0:
                    ds_fm2.wait_all(em, "tensor")
                wait("tensor", cv, max(v_kvr, oset_free[s]))
                for p_ in range(8):
                    bank = 3 * s + p_ // 3
                    off = 130 * (p_ % 3)
                    if p_ == 7:
                        t_o = emi("tensor", lambda e, t=tt, p2=p_, b_=bank, o=off: e.matmul(
                            psO[b_][:, o:o + 130], qp_fm[:, p2, P * t:P * (t + 1)],
                            kvr[:, 130 * p2:130 * (p2 + 1)], start=True, stop=True), ct)
                    else:
                        em("tensor", lambda e, t=tt, p2=p_, b_=bank, o=off: e.matmul(
                            psO[b_][:, o:o + 130], qp_fm[:, p2, P * t:P * (t + 1)],
                            kvr[:, 130 * p2:130 * (p2 + 1)], start=True, stop=True))
                wait("vector", ct, t_o)
                ds_a.wait(em, "vector", astore[s])
                em("vector", lambda e, s_=s: e.tensor_copy(
                    nrm16[:, 0:6].rearrange("p (a b c) -> p a b c", a=3, b=2),
                    psO[3 * s_][:, 0:390].rearrange("p (a b c) -> p a b c", a=3, c=65)[:, :, :, 64:65]))
                em("vector", lambda e, s_=s: e.tensor_copy(
                    nrm16[:, 6:12].rearrange("p (a b c) -> p a b c", a=3, b=2),
                    psO[3 * s_ + 1][:, 0:390].rearrange("p (a b c) -> p a b c", a=3, c=65)[:, :, :, 64:65]))
                em("vector", lambda e, s_=s: e.tensor_copy(
                    nrm16[:, 12:16].rearrange("p (a b c) -> p a b c", a=2, b=2),
                    psO[3 * s_ + 2][:, 0:260].rearrange("p (a b c) -> p a b c", a=2, c=65)[:, :, :, 64:65]))
                vsp()
                em("vector", lambda e: e.tensor_scalar_add(nrm2[:, :], nrm16[:, :], EPS))
                vsp()
                em("vector", lambda e: e.reciprocal(rn16[:, :], nrm2[:, :]))
                vsp()
                for p_ in range(8):
                    bank = 3 * s + p_ // 3
                    off = 130 * (p_ % 3)
                    for h2 in range(2):
                        if p_ == 7 and h2 == 1:
                            oset_free[s] = emi("vector", lambda e, s_=s, p2=p_, b_=bank, o=off, h_=h2: e.tensor_single_scalar(
                                a_blk[:, s_, 64 * (2 * p2 + h_):64 * (2 * p2 + h_ + 1)],
                                psO[b_][:, o + 65 * h_:o + 65 * h_ + 64],
                                rn16[:, 2 * p2 + h_:2 * p2 + h_ + 1], ALU.mult), cv)
                        else:
                            em("vector", lambda e, s_=s, p2=p_, b_=bank, o=off, h_=h2: e.tensor_single_scalar(
                                a_blk[:, s_, 64 * (2 * p2 + h_):64 * (2 * p2 + h_ + 1)],
                                psO[b_][:, o + 65 * h_:o + 65 * h_ + 64],
                                rn16[:, 2 * p2 + h_:2 * p2 + h_ + 1], ALU.mult))
                wait("gpsimd", cv, oset_free[s])
                astore[s] = ds_a.issue(em, "gpsimd", lambda e, t=tt, s_=s: e.dma_start(
                    d_a[P * t:P * (t + 1), :], a_blk[:, s_, :]), tt)

            for tt in range(NT):
                a2_iter(2 * tt)
                a2_iter(2 * tt + 1)
                b1_iter(tt)
            barrier()

        if stop_after in ("A2", "B1"):
            return _finish(nc, block, ops, barrier, _cleanup)
        with ExitStack() as b2s:
            en = b2s.enter_context
            wdT_sb = en(nc.sbuf_tensor("wdT_sb", [P, NJC, D], BF16))
            projw_sb = en(nc.sbuf_tensor("projw_sb", [P, DC, D], BF16))
            a_fm = en(nc.sbuf_tensor("a_fm", [P, 2, DC, 512], BF16))
            hq_fm = en(nc.sbuf_tensor("hq_fm", [P, 2, NJC, 512], BF16))
            o1_s = en(nc.sbuf_tensor("o1_s", [P, 2, D], F32))
            t2_s = en(nc.sbuf_tensor("t2_s", [P, 2, D], F32))
            x2_t = en(nc.sbuf_tensor("x2_t", [P, 2, D], F32))
            psD = [en(nc.psum_tensor(f"psD{i}", [P, D], F32)) for i in range(2)]
            psP = [en(nc.psum_tensor(f"psP{i}", [P, D], F32)) for i in range(2)]

            wqd_v = d_wqd.rearrange("(a p) c -> p a c", p=P)
            wait("sync", ccs, ccs_gathers)
            for g4 in range(4):
                ds_wd.issue(em, "sync", lambda e, g_=g4: e.dma_start(
                    wdT_sb[:, 8 * g_:8 * (g_ + 1), :], wqd_v[:, 8 * g_:8 * (g_ + 1), :]))
            projw_v = dt_projwT.rearrange("(a p) c -> p a c", p=P)
            for dc in range(DC):
                ds_pj.issue(em, "gpsimd", lambda e, d=dc: e.dma_start(
                    projw_sb[:, d, :], projw_v[:, d, :]))

            b2_free = [0, 0]
            b2_tok = [None, None]
            psfree2 = [0, 0]
            psPfree = [0, 0]
            ostore = [None, None]

            def b2_loads(ss):
                sb = ss % 2
                wait("sync", ct, b2_free[sb])
                for jc in range(NJC):
                    ds_b2.issue(em, "sync", lambda e, ss_=ss, j=jc, s_=sb: e.dma_start(
                        hq_fm[:, s_, j, :],
                        d_hq[512 * ss_:512 * (ss_ + 1), P * j:P * (j + 1)],
                        transpose=True), ss)
                for dc in range(DC):
                    b2_tok[sb] = ds_b2.issue(em, "sync", lambda e, ss_=ss, d=dc, s_=sb: e.dma_start(
                        a_fm[:, s_, d, :],
                        d_a[512 * ss_:512 * (ss_ + 1), P * d:P * (d + 1)],
                        transpose=True), ss)

            def b2_compute(ss):
                sb = ss % 2
                for tl in range(NB):
                    tt = NB * ss + tl
                    j = tt % 2
                    wait("sync", cv, psPfree[j])
                    x2_tok = ds_x2.issue(em, "sync", lambda e, t=tt, j_=j: e.dma_start(
                        x2_t[:, j_, :], dt_x[P * t:P * (t + 1), :]), tt)
                    if ss == 0 and tl == 0:
                        ds_wd.wait_all(em, "tensor")
                        ds_pj.wait_all(em, "tensor")
                    ds_b2.wait(em, "tensor", b2_tok[sb])
                    wait("tensor", ca, psfree2[j])
                    wait("tensor", cv, psPfree[j])
                    for jc in range(NJC):
                        for hf in range(2):
                            em("tensor", lambda e, jj=jc, t=tl, j_=j, h_=hf, s_=sb: e.matmul(
                                psD[j_][:, 512 * h_:512 * (h_ + 1)],
                                hq_fm[:, s_, jj, P * t:P * (t + 1)],
                                wdT_sb[:, jj, 512 * h_:512 * (h_ + 1)],
                                start=(jj == 0), stop=(jj == NJC - 1)))
                    for dc in range(DC):
                        for hf in range(2):
                            if dc == DC - 1 and hf == 1:
                                t_b2 = emi("tensor", lambda e, d=dc, t=tl, j_=j, h_=hf, s_=sb: e.matmul(
                                    psP[j_][:, 512 * h_:512 * (h_ + 1)],
                                    a_fm[:, s_, d, P * t:P * (t + 1)],
                                    projw_sb[:, d, 512 * h_:512 * (h_ + 1)],
                                    start=(d == 0), stop=(d == DC - 1)), ct)
                            else:
                                em("tensor", lambda e, d=dc, t=tl, j_=j, h_=hf, s_=sb: e.matmul(
                                    psP[j_][:, 512 * h_:512 * (h_ + 1)],
                                    a_fm[:, s_, d, P * t:P * (t + 1)],
                                    projw_sb[:, d, 512 * h_:512 * (h_ + 1)],
                                    start=(d == 0), stop=(d == DC - 1)))
                    if tl == NB - 1:
                        b2_free[sb] = t_b2
                    wait("scalar", ct, t_b2)
                    wait("scalar", cv, psPfree[j])
                    psfree2[j] = emi("scalar", lambda e, t=tt, j_=j: e.activation(
                        o1_s[:, j_, :], psD[j_][:, :], AF.Copy, scale=cdown[:, t:t + 1]), ca)
                    wait("vector", ca, psfree2[j])
                    ds_x2.wait(em, "vector", x2_tok)
                    ds_out.wait(em, "vector", ostore[j])
                    em("vector", lambda e, j_=j: e.tensor_tensor(
                        t2_s[:, j_, :], o1_s[:, j_, :], psP[j_][:, :], ALU.add))
                    em("vector", lambda e, j_=j: e.tensor_tensor(
                        o1_s[:, j_, :], t2_s[:, j_, :], pbias[:, :], ALU.add))
                    psPfree[j] = emi("vector", lambda e, j_=j: e.tensor_tensor(
                        t2_s[:, j_, :], o1_s[:, j_, :], x2_t[:, j_, :], ALU.add), cv)
                    wait("gpsimd", cv, psPfree[j])
                    ostore[j] = ds_out.issue(em, "gpsimd", lambda e, t=tt, j_=j: e.dma_start(
                        dt_out[P * t:P * (t + 1), :], t2_s[:, j_, :]), tt)

            for ss in range(NB):
                b2_loads(ss)
                if ss >= 1:
                    b2_compute(ss - 1)
            b2_compute(NB - 1)
            ds_out.wait_all(em, "gpsimd")
            barrier()

        return _finish(nc, block, ops, barrier, _cleanup)


_NC_CACHE = {}


def kernel(L0_attn_x, L0_attn_qkv_w, L0_attn_qkv_b, L0_attn_proj_w, L0_attn_proj_b,
           L0_attn_projection_matrix, L0_ffn_w_gate, L0_ffn_w_up, L0_ffn_w_down,
           debug=False):
    x = np.ascontiguousarray(np.asarray(L0_attn_x, dtype=np.float32).reshape(16384, D))
    qkv_w = np.asarray(L0_attn_qkv_w, dtype=np.float32)
    qkv_b = np.asarray(L0_attn_qkv_b, dtype=np.float32)
    proj_w = np.asarray(L0_attn_proj_w, dtype=np.float32)
    proj_b = np.asarray(L0_attn_proj_b, dtype=np.float32)
    pm = np.asarray(L0_attn_projection_matrix, dtype=np.float32)
    w_gate = np.asarray(L0_ffn_w_gate, dtype=np.float32)
    w_up = np.asarray(L0_ffn_w_up, dtype=np.float32)
    w_down = np.asarray(L0_ffn_w_down, dtype=np.float32)

    qkvwT = np.ascontiguousarray(qkv_w.T)
    qkbc = np.ascontiguousarray(qkv_b.reshape(24, P).T)
    projwT = np.ascontiguousarray(proj_w.T)
    pmT = np.ascontiguousarray(pm.T)
    wgT = np.ascontiguousarray(w_gate.T)
    wuT = np.ascontiguousarray(w_up.T)
    wdT = np.ascontiguousarray(w_down.T)
    qkvb2 = np.ascontiguousarray(qkv_b.reshape(1, 3 * D))
    projb2 = np.ascontiguousarray(proj_b.reshape(1, D))

    key = bool(debug)
    if key not in _NC_CACHE:
        _NC_CACHE[key] = build(debug=debug)
    nc = _NC_CACHE[key]

    in_maps = []
    for c in range(NCORES):
        in_maps.append({
            "x": np.ascontiguousarray(x[T * c:T * (c + 1), :]),
            "qkvwT": qkvwT, "qkv_b": qkvb2, "qkb_cols": qkbc,
            "projwT": projwT, "proj_b": projb2, "pmT": pmT,
            "wg_mine": np.ascontiguousarray(wgT[P * c:P * (c + 1), :]),
            "wu_mine": np.ascontiguousarray(wuT[P * c:P * (c + 1), :]),
            "wd_mine": np.ascontiguousarray(wdT[4 * P * c:4 * P * (c + 1), :]),
        })
    res = run_bass_kernel_spmd(nc, in_maps, core_ids=list(range(NCORES)))
    out = np.concatenate([np.asarray(r["out"]) for r in res.results], axis=0)
    if debug:
        return out.reshape(1, 16384, D), res
    return out.reshape(1, 16384, D)
